# version 25
# baseline (speedup 1.0000x reference)
import math
import numpy as np
from contextlib import ExitStack

import concourse.bass as bass
import concourse.mybir as mybir
from concourse.bass_utils import run_bass_kernel_spmd

F32 = mybir.dt.float32
BF16 = mybir.dt.bfloat16
AF = mybir.ActivationFunctionType
ALU = mybir.AluOpType
AX = mybir.AxisListType


class Cfg:
    def __init__(self, D=4096, QR=1536, KVR=512, NH=32, RANK=1024,
                 T_own=1024, T_batch=4096):
        self.D, self.QR, self.KVR, self.NH, self.RANK = D, QR, KVR, NH, RANK
        self.T_own, self.T_batch = T_own, T_batch
        self.RD, self.ND, self.VD = 64, 128, 128
        self.QK = self.ND + self.RD
        self.DC = D // 128
        self.QRC = QR // 128
        self.KVC = KVR // 128
        self.RC = RANK // 128
        self.TSB = T_batch // 128
        self.TSO = T_own // 128
        self.NQC = T_own // 512
        self.KS = T_batch // 128
        self.NJ = D // 512
        self.KVW = KVR + self.RD
        self.HW = self.ND + self.RD
        self.KW = self.ND + self.VD
        self.EPS_LN = 1e-5
        self.EPS_RMS = 1e-6
        self.SCALE = self.QK ** (-0.5)
        assert T_own % 512 == 0 and T_batch % 256 == 0


FULL = Cfg()


def _nchunks(total, step=512):
    out = []
    off = 0
    while off < total:
        out.append((off, min(step, total - off)))
        off += step
    return out


class Plan:

    ENGINES = ("sync", "gpsimd", "vector", "scalar", "tensor")

    def __init__(self, nc, ctx, sem_names):
        self.nc = nc
        self.sems = {}
        self.c = {}
        for s in sem_names:
            self.sems[s] = ctx.enter_context(nc.semaphore(s))
            self.c[s] = 0
        self.sched = {e: [] for e in self.ENGINES}

    OWN = {"vector": "V", "scalar": "S"}

    def op(self, eng, build, waits=(), inc=None, incn=1):
        own = self.OWN.get(eng)
        if own is not None:
            waits = tuple(waits) + ((own, self.c[own]),)
            if inc is None:
                inc = own
        waits = tuple((s, v) for (s, v) in waits if v > 0)
        sems = self.sems

        def run(e, waits=waits, build=build, inc=inc, incn=incn):
            for s, v in waits:
                e.wait_ge(sems[s], v)
            ins = build(e)
            if inc is not None:
                ins.then_inc(sems[inc], incn)
        self.sched[eng].append(run)
        if inc is not None:
            self.c[inc] += incn
            return self.c[inc]
        return None

    def wait(self, eng, *waits):
        waits = tuple((s, v) for (s, v) in waits if v > 0)
        if not waits:
            return
        sems = self.sems

        def run(e, waits=waits):
            for s, v in waits:
                e.wait_ge(sems[s], v)
        self.sched[eng].append(run)

    def barrier(self):
        snap = tuple((s, v) for s, v in self.c.items() if v > 0)
        for e in self.ENGINES:
            self.wait(e, *snap)


def build_program(cfg: Cfg, phases=4):
    nc = bass.Bass("TRN2", target_bir_lowering=False, debug=False)
    C = cfg
    DT = {}

    def din(name, shape):
        DT[name] = nc.dram_tensor(name, list(shape), F32, kind="ExternalInput").ap()
        return DT[name]

    x = din("x", (C.T_batch, C.D))
    w_qa = din("w_qa", (C.D, C.QR))
    g_qa = din("g_qa", (C.QR,))
    w_qb = din("w_qb", (C.QR, C.NH * C.HW))
    w_kva = din("w_kva", (C.D, C.KVW))
    g_kva = din("g_kva", (C.KVR,))
    w_kvb = din("w_kvb", (C.KVR, C.NH * C.KW))
    w_o = din("w_o", (C.NH * C.VD, C.D))
    U = din("U", (C.D, C.RANK))
    Sv = din("Sv", (C.RANK,))
    V = din("V", (C.D, C.RANK))
    cosT = din("cosT", (C.RD, C.T_batch))
    sinT = din("sinT", (C.RD, C.T_batch))
    ident = din("ident", (128, 128))
    ones = din("ones", (128, 128))
    perm = din("perm", (64, 64))
    out = nc.dram_tensor("out", [C.T_own, C.D], F32, kind="ExternalOutput").ap()

    with ExitStack() as top:
        E = top.enter_context
        P = Plan(nc, top, ["qs0", "qs1", "qs2", "qs3", "qg0", "qg1", "qg2", "qg3",
                           "qg4", "qg5", "qc0", "qc1", "qo0", "qo1",
                           "T", "V", "S"])

        ident_bf = E(nc.sbuf_tensor("ident_bf", [128, 128], BF16, side="left"))
        ones_bf = E(nc.sbuf_tensor("ones_bf", [128, 128], BF16, side="left"))
        ones_f32 = E(nc.sbuf_tensor("ones_f32", [128, 128], F32, side="left"))
        perm_bf = E(nc.sbuf_tensor("perm_bf", [64, 64], BF16, side="left"))
        h1sT = E(nc.sbuf_tensor("h1sT", [128, C.RC * C.T_own], BF16, side="left"))
        ph123 = ExitStack()
        E123 = ph123.enter_context
        cosQ = E123(nc.sbuf_tensor("cosQ", [64, C.T_own], BF16, side="right"))
        sinQ = E123(nc.sbuf_tensor("sinQ", [64, C.T_own], BF16, side="right"))
        ropeP = E123(nc.sbuf_tensor("ropeP", [65, C.T_batch], BF16, side="right"))
        k_cnT = E123(nc.sbuf_tensor("k_cnT", [128, C.KVC * C.T_batch], BF16,
                                    side="right"))

        def cast_dma(dst_ap, src_ap):
            return lambda e: e.dma_start(out=dst_ap, in_=src_ap)

        P.op("gpsimd", cast_dma(ident_bf[:], ident[:, :]), inc="qc0", incn=16)
        P.op("gpsimd", cast_dma(ones_bf[:], ones[:, :]), inc="qc0", incn=16)
        P.op("gpsimd", cast_dma(perm_bf[:], perm[:, :]), inc="qc0", incn=16)
        P.op("gpsimd", lambda e: e.dma_start(out=ones_f32[:], in_=ones[:, :]),
             inc="qc0", incn=16)
        P.op("vector", lambda e: e.memset(ropeP[64:65, :], 1.0), inc="V")
        const_vals = [C.EPS_LN, 1.0 / C.KVR, 1.0 / C.QR, C.SCALE, 1.0, 0.0,
                      1.0 / C.D, C.EPS_RMS, -1.0, -0.5]
        const_sb = E(nc.sbuf_tensor("const_sb", [128, len(const_vals)], F32,
                                    side="left"))
        cv = 0
        for i, val in enumerate(const_vals):
            cv = P.op("vector", lambda e, i=i, val=val: e.memset(
                const_sb[:, i:i + 1], val), inc="V")
            nc.const_aps.register(F32, val, const_sb[:, i:i + 1])
        P.wait("scalar", ("V", cv))
        P.wait("vector", ("V", cv))
        P.wait("tensor", ("V", cv))

        with ExitStack() as ph1:
            E1 = ph1.enter_context
            cosT_bf = E1(nc.sbuf_tensor("cosT_bf", [64, C.T_batch], BF16,
                                        side="right"))
            sinT_bf = E1(nc.sbuf_tensor("sinT_bf", [64, C.T_batch], BF16,
                                        side="right"))
            w_kva_bf = E1(nc.sbuf_tensor("w_kva_bf", [128, C.DC * C.KVW], BF16,
                                         side="right"))
            g_kvaB = E1(nc.sbuf_tensor("g_kvaB", [128, C.KVR], F32, side="right"))
            xf = [E1(nc.sbuf_tensor(f"xf{i}", [128, C.D], BF16, side="right"))
                  for i in range(4)]
            xn_bf = [E1(nc.sbuf_tensor(f"xn{i}", [128, C.D], BF16, side="right"))
                     for i in range(2)]
            xnT_tmp = [E1(nc.sbuf_tensor(f"xnTt{i}", [128, C.DC * 128], BF16,
                                         side="right")) for i in range(2)]
            kvf = E1(nc.sbuf_tensor("kvf", [128, C.KVW], F32, side="right"))
            kvn_bf = [E1(nc.sbuf_tensor(f"kvn_bf{j}", [128, C.KVC * 128 + 128],
                                        BF16, side="right")) for j in range(2)]
            kc_tmp = E1(nc.sbuf_tensor("kc_tmp", [128, C.KVR], BF16, side="right"))
            kropeT = E1(nc.sbuf_tensor("kropeT", [128, C.T_batch], BF16, side="right"))
            ropetmp = E1(nc.sbuf_tensor("ropetmp", [64, C.T_batch], BF16, side="right"))
            stat = E1(nc.sbuf_tensor("stat", [128, 16], F32, side="right"))
            bnst = E1(nc.sbuf_tensor("bnst", [128, 48], F32, side="right"))
            kvps = [E1(nc.psum_tensor(f"kvps{i}", [128, C.KVW], F32)) for i in range(2)]
            trps = [E1(nc.psum_tensor(f"trps{i}", [128, 512], F32)) for i in range(2)]
            kcn_ps = E1(nc.psum_tensor("kcnps", [128, C.KVC * 128], F32))
            krp_ps = E1(nc.psum_tensor("krpps", [128, 128], F32))

            P.op("gpsimd", cast_dma(cosT_bf[:], cosT[:, :]), inc="qc0", incn=16)
            P.op("gpsimd", cast_dma(sinT_bf[:], sinT[:, :]), inc="qc0", incn=16)
            P.op("gpsimd", cast_dma(
                w_kva_bf[:].rearrange("p (c m) -> p c m", c=C.DC),
                w_kva.rearrange("(c p) m -> p c m", p=128)), inc="qc0", incn=16)
            P.op("gpsimd", cast_dma(
                g_kvaB[:], g_kva[None, :].broadcast_to((128, C.KVR))),
                inc="qc0", incn=16)
            qw_const = P.c["qc0"]
            P.op("vector", lambda e: e.memset(
                kvn_bf[0][:, C.KVC * 128 + C.RD:], 0.0), inc="V")
            P.op("vector", lambda e: e.memset(
                kvn_bf[1][:, C.KVC * 128 + C.RD:], 0.0), inc="V")

            xf_free = [(), (), (), ()]
            xn_free = [(), ()]
            xnTt_free = [(), ()]
            kvps_free = [(), ()]
            trps_free = [(), ()]

            kvn_ready_v = 0
            kvn_ready_s = 0
            kvn_free = [(), ()]
            kv_done = {}
            last_ev_kr = 0

            kcn_free = ()
            krp_free = ()

            def emit_kcn_tr(t):
                nonlocal last_ev_kr, kcn_free, krp_free
                i2 = t % 2
                v_kc_t, s_kr_t = kv_done[t]
                t_tkc = None
                for c in range(C.KVC):
                    t_tkc = P.op("tensor", lambda e, c=c, i2=i2: e.matmul(
                        kcn_ps[:, c * 128:(c + 1) * 128],
                        kvn_bf[i2][:, c * 128:(c + 1) * 128], ident_bf[:],
                        start=(c == 0), stop=(c == C.KVC - 1),
                        skip_group_check=True),
                        waits=[("V", v_kc_t), ("S", s_kr_t)]
                        + (list(kcn_free) if c == 0 else []),
                        inc="T" if c == C.KVC - 1 else None)
                ev_kc = P.op("vector", lambda e, t=t: e.tensor_copy(
                    k_cnT[:].rearrange("p (c n) -> p c n", c=C.KVC)
                    [:, :, t * 128:(t + 1) * 128],
                    kcn_ps[:].rearrange("p (c n) -> p c n", c=C.KVC)),
                    waits=[("T", t_tkc)], inc="V")
                kcn_free = (("V", ev_kc),)
                t_trr = P.op("tensor", lambda e, i2=i2: e.matmul(
                    krp_ps[:, 0:128],
                    kvn_bf[i2][:, C.KVC * 128:C.KVC * 128 + 128], ident_bf[:],
                    start=True, stop=True),
                    waits=list(krp_free), inc="T")
                ev_kr = P.op("vector", lambda e, t=t: e.tensor_copy(
                    kropeT[:, t * 128:(t + 1) * 128], krp_ps[:, 0:128]),
                    waits=[("T", t_trr)], inc="V")
                krp_free = (("V", ev_kr),)
                kvn_free[i2] = (("T", t_trr),)
                last_ev_kr = ev_kr

            norm_of = {}
            qx_of = {}
            sd_of = {}
            v5_of = {}
            NG = C.D // 512
            HG = [g for g in range(NG) if (g + 1) * 512 <= C.D // 2]

            def emit_dma(t):
                i = t % 4
                qx_of[t] = P.op("gpsimd", lambda e, t=t, i=i: e.dma_start(
                    out=xf[i][:], in_=x[t * 128:(t + 1) * 128, :]),
                    waits=xf_free[i], inc=f"qg{i}", incn=16)

            def emit_bn(t, g):
                i = t % 4
                P.op("vector", lambda e, i=i, g=g: e.bn_stats(
                    bnst[:, 6 * g:6 * g + 6],
                    xf[i][:, 512 * g:512 * g + 512]),
                    waits=[(f"qg{i}", qx_of[t])], inc="V")

            def emit_agg(t):
                B = 8 * (t % 2)
                v_ag = P.op("vector", lambda e, B=B: e.bn_aggr(
                    stat[:, B + 0:B + 2], bnst[:, 0:6 * NG]),
                    waits=[("S", sd_of.get(t - 2, 0))], inc="V")
                s_sd = P.op("scalar", lambda e, B=B: e.activation(
                    stat[:, B + 2:B + 3], stat[:, B + 1:B + 2], AF.Sqrt,
                    bias=C.EPS_LN),
                    waits=[("V", v_ag), ("V", v5_of.get(t - 2, 0))], inc="S")
                sd_of[t] = s_sd
                v5 = P.op("vector", lambda e, B=B: e.reciprocal(
                    stat[:, B + 3:B + 4], stat[:, B + 2:B + 3]),
                    waits=[("S", s_sd),
                           ("S", norm_of[t - 2][1] if t >= 2 else 0)], inc="V")
                v5_of[t] = v5
                P.op("vector", lambda e, B=B: e.scalar_tensor_tensor(
                    stat[:, B + 4:B + 5], stat[:, B + 0:B + 1], -1.0,
                    stat[:, B + 3:B + 4], ALU.mult, ALU.mult), inc="V")

            def emit_norm1(t):
                i = t % 4
                i2 = t % 2
                B = 8 * (t % 2)
                vv = P.c["V"]
                s_a = P.op("scalar", lambda e, i=i, i2=i2, B=B: e.activation(
                    xn_bf[i2][:, 0:C.D // 2], xf[i][:, 0:C.D // 2], AF.Identity,
                    scale=stat[:, B + 3:B + 4], bias=stat[:, B + 4:B + 5]),
                    waits=[("V", vv)] + list(xn_free[i2]), inc="S")
                s_b = P.op("scalar", lambda e, i=i, i2=i2, B=B: e.activation(
                    xn_bf[i2][:, C.D // 2:C.D], xf[i][:, C.D // 2:C.D],
                    AF.Identity,
                    scale=stat[:, B + 3:B + 4], bias=stat[:, B + 4:B + 5]),
                    inc="S")
                norm_of[t] = (s_a, s_b)
                xf_free[i] = (("S", s_b),)

            emit_dma(0)
            for g in range(NG):
                emit_bn(0, g)
            emit_agg(0)
            emit_norm1(0)
            for t in range(C.TSB):
                i2 = t % 2
                if t + 1 < C.TSB:
                    emit_dma(t + 1)
                s_norm_a, s_norm_b = norm_of[t]
                ev_counts = []
                for g in range(C.DC // 4):
                    b = g % 2
                    sn = s_norm_a if g in HG else s_norm_b
                    t_tr = None
                    for k in range(4):
                        c = g * 4 + k
                        t_tr = P.op("tensor", lambda e, i2=i2, c=c, b=b, k=k:
                             e.matmul(trps[b][:, (c % 4) * 128:(c % 4) * 128 + 128],
                                      xn_bf[i2][:, c * 128:(c + 1) * 128],
                                      ident_bf[:], start=(k == 0), stop=(k == 3),
                                      skip_group_check=True),
                             waits=([("S", sn), ("qc0", qw_const)]
                                    + (list(trps_free[b]) if k == 0 else [])),
                             inc="T" if k == 3 else None)
                    if g % 2 == 0:
                        ev = P.op("vector", lambda e, i2=i2, g=g, b=b:
                                  e.tensor_copy(
                            xnT_tmp[i2][:, g * 512:(g + 1) * 512], trps[b][:]),
                            waits=[("T", t_tr)]
                            + list(xnTt_free[i2] if g == 0 else ()), inc="V")
                        ev = ("V", ev)
                    else:
                        ev = P.op("scalar", lambda e, i2=i2, g=g, b=b:
                                  e.activation(
                            xnT_tmp[i2][:, g * 512:(g + 1) * 512], trps[b][:],
                            AF.Identity),
                            waits=[("T", t_tr)], inc="S")
                        ev = ("S", ev)
                    trps_free[b] = (ev,)
                    ev_counts.append(ev)
                    if t + 1 < C.TSB and g < NG:
                        emit_bn(t + 1, g)
                xn_free[i2] = (ev_counts[-1],)
                if t + 1 < C.TSB:
                    emit_agg(t + 1)
                    emit_norm1(t + 1)
                kvch = _nchunks(C.KVW)
                t_kv = None
                for ci, (noff, nlen) in enumerate(kvch):
                    for c in range(C.DC):
                        need_ev = ev_counts[c // 4]
                        last = (ci == len(kvch) - 1 and c == C.DC - 1)
                        t_kv = P.op("tensor", lambda e, i2=i2, c=c, noff=noff,
                                    nlen=nlen:
                             e.matmul(kvps[i2][:, noff:noff + nlen],
                                      xnT_tmp[i2][:, c * 128:(c + 1) * 128],
                                      w_kva_bf[:, c * C.KVW + noff:
                                               c * C.KVW + noff + nlen],
                                      start=(c == 0), stop=(c == C.DC - 1)),
                             waits=([need_ev] if c % 4 == 0 else [])
                             + (list(kvps_free[i2]) if (c == 0 and noff == 0)
                                else []),
                             inc="T" if last else None)
                xnTt_free[i2] = (("T", t_kv),)
                B = 8 * (t % 2)
                s_kvf = P.op("scalar", lambda e, i2=i2: e.activation(
                    kvf[:], kvps[i2][:], AF.Identity),
                    waits=[("T", t_kv), ("V", kvn_ready_v), ("S", kvn_ready_s)],
                    inc="S")
                kvps_free[i2] = (("S", s_kvf),)
                v_ssq = P.op("vector", lambda e, B=B: e.scalar_tensor_tensor(
                    kc_tmp[:], kvf[:, 0:C.KVR], 1.0, kvf[:, 0:C.KVR],
                    ALU.mult, ALU.mult, accum_out=stat[:, B + 5:B + 6]),
                    waits=[("S", s_kvf)], inc="V")
                s_rms = P.op("scalar", lambda e, B=B: e.activation(
                    stat[:, B + 6:B + 7], stat[:, B + 5:B + 6], AF.Sqrt,
                    scale=1.0 / C.KVR),
                    waits=[("V", v_ssq)], inc="S")
                v_eps = P.op("vector", lambda e, B=B: e.tensor_scalar(
                    stat[:, B + 6:B + 7], stat[:, B + 6:B + 7], C.EPS_RMS, None,
                    ALU.add),
                    waits=[("S", s_rms)], inc="V")
                v_rin = P.op("vector", lambda e, B=B: e.reciprocal(
                    stat[:, B + 7:B + 8], stat[:, B + 6:B + 7]), inc="V")
                s_kc = P.op("scalar", lambda e, B=B: e.activation(
                    kc_tmp[:], kvf[:, 0:C.KVR], AF.Identity,
                    scale=stat[:, B + 7:B + 8]),
                    waits=[("V", v_rin)], inc="S")
                v_kc = P.op("vector", lambda e, i2=i2: e.tensor_tensor(
                    kvn_bf[i2][:, 0:C.KVR], kc_tmp[:], g_kvaB[:], ALU.mult),
                    waits=[("S", s_kc)] + list(kvn_free[i2]), inc="V")
                s_kr = P.op("scalar", lambda e, i2=i2: e.activation(
                    kvn_bf[i2][:, C.KVR:C.KVR + C.RD],
                    kvf[:, C.KVR:C.KVR + C.RD], AF.Identity), inc="S")
                kvn_ready_s = s_kr
                kvn_ready_v = v_ssq
                kv_done[t] = (v_kc, s_kr)
                if t > 0:
                    emit_kcn_tr(t - 1)
            emit_kcn_tr(C.TSB - 1)

            permd = []
            for (noff, nlen) in _nchunks(C.T_batch):
                t_p = P.op("tensor", lambda e, noff=noff, nlen=nlen: e.matmul(
                    trps[0][0:64, 0:nlen], perm_bf[:],
                    kropeT[0:64, noff:noff + nlen], start=True, stop=True),
                    waits=[("V", last_ev_kr)] + list(trps_free[0]), inc="T")
                ev = P.op("vector", lambda e, noff=noff, nlen=nlen: e.tensor_copy(
                    ropetmp[:, noff:noff + nlen], trps[0][0:64, 0:nlen]),
                    waits=[("T", t_p)], inc="V")
                trps_free[0] = (("V", ev),)
            v_a = P.op("vector", lambda e: e.tensor_tensor(
                ropeP[0:64, :], kropeT[0:64, :], cosT_bf[:], ALU.mult),
                waits=[("V", P.c["V"]), ("qc0", qw_const)], inc="V")
            v_b = P.op("vector", lambda e: e.tensor_tensor(
                ropetmp[:], ropetmp[:], sinT_bf[:], ALU.mult), inc="V")
            v_rp = P.op("vector", lambda e: e.tensor_tensor(
                ropeP[0:64, :], ropeP[0:64, :], ropetmp[:], ALU.add), inc="V")
            P.op("vector", lambda e: e.tensor_copy(
                cosQ[:], cosT_bf[:, 0:C.T_own]), inc="V")
            P.op("vector", lambda e: e.tensor_copy(
                sinQ[:], sinT_bf[:, 0:C.T_own]), inc="V")
            P.barrier()

        if phases < 2:
            ph123.close()
            _early_exit(P, nc, out, const_sb)
            return _finish(nc, top, P)
        with ExitStack() as ph2:
            E2 = ph2.enter_context
            q_cnT = E123(nc.sbuf_tensor("q_cnT", [128, C.QRC * C.T_own], BF16,
                                        side="right"))
            rstdB = E123(nc.sbuf_tensor("rstdB", [128, C.T_own], F32,
                                        side="right"))
            g_qaT = E2(nc.sbuf_tensor("g_qaT", [128, C.QRC], F32, side="right"))
            SvT = E2(nc.sbuf_tensor("SvT", [128, C.RC], F32, side="right"))
            xnT = E2(nc.sbuf_tensor("xnT", [128, C.DC * C.T_own], BF16, side="right"))
            wcol = [E2(nc.sbuf_tensor(f"wcol{i}", [128, C.DC * 128], BF16,
                                      side="right")) for i in range(2)]
            xf2 = [E2(nc.sbuf_tensor(f"p2xf{i}", [128, C.D], BF16, side="right"))
                  for i in range(3)]
            xn2 = [E2(nc.sbuf_tensor(f"p2xn{i}", [128, C.D], BF16, side="right"))
                     for i in range(1)]
            qsq = E2(nc.sbuf_tensor("qsq", [128, 512], BF16, side="right"))
            rrow = E2(nc.sbuf_tensor("rrow", [1, C.T_own], F32, side="right"))
            stat2 = E2(nc.sbuf_tensor("p2stat", [128, 8], F32, side="right"))
            bnst2 = E2(nc.sbuf_tensor("p2bnst", [128, 48], F32, side="right"))
            qct_ps = [E2(nc.psum_tensor(f"qctps{i}", [128, 512], F32))
                      for i in range(2)]
            trps2 = [E2(nc.psum_tensor(f"p2trps{i}", [128, 512], F32))
                    for i in range(2)]
            ssq_ps = E2(nc.psum_tensor("ssqps", [65, C.T_own], F32))
            rB_ps = E2(nc.psum_tensor("rBps", [128, C.T_own], F32))

            P.op("sync", lambda e: e.dma_start(
                out=g_qaT[:], in_=g_qa.rearrange("(c p) -> p c", p=128),
                allow_slow_non_contiguous=True),
                inc="qc1", incn=16)
            P.op("sync", lambda e: e.dma_start(
                out=SvT[:], in_=Sv.rearrange("(c p) -> p c", p=128),
                allow_slow_non_contiguous=True),
                inc="qc1", incn=16)
            qx_cst = P.c["qc1"]

            xf_free = [(), (), ()]
            xn_free = [(), ()]
            trps_free = [(), ()]
            xnT_ev = []
            norm_of2 = {}
            v6_of2 = {}
            prev_norm2 = [0]
            wcol0_ssq = [0]

            def emit_ln2(t):
                i = t % 3
                i2 = 0
                qx_t = P.op("gpsimd", lambda e, t=t, i=i: e.dma_start(
                    out=xf2[i][:], in_=x[t * 128:(t + 1) * 128, :]),
                    waits=xf_free[i], inc=f"qg{i}", incn=16)
                for g in range(C.D // 512):
                    P.op("vector", lambda e, i=i, g=g: e.bn_stats(
                        bnst2[:, 6 * g:6 * g + 6],
                        xf2[i][:, 512 * g:512 * g + 512]),
                        waits=[(f"qg{i}", qx_t)], inc="V")
                v_ag = P.op("vector", lambda e: e.bn_aggr(
                    stat2[:, 0:2], bnst2[:, 0:6 * (C.D // 512)]),
                    waits=[("S", prev_norm2[0])], inc="V")
                s_sd = P.op("scalar", lambda e: e.activation(
                    stat2[:, 3:4], stat2[:, 1:2], AF.Sqrt,
                    bias=C.EPS_LN), waits=[("V", v_ag)], inc="S")
                P.op("vector", lambda e: e.reciprocal(
                    stat2[:, 5:6], stat2[:, 3:4]),
                    waits=[("S", s_sd)], inc="V")
                v6 = P.op("vector", lambda e: e.scalar_tensor_tensor(
                    stat2[:, 6:7], stat2[:, 0:1], -1.0, stat2[:, 5:6],
                    ALU.mult, ALU.mult), inc="V")
                v6_of2[t] = v6
                wcol0_ssq[0] = v6

            def emit_norm2(t):
                i = t % 3
                i2 = 0
                s_norm = P.op("scalar", lambda e, i=i, i2=i2: e.activation(
                    xn2[i2][:], xf2[i][:], AF.Identity,
                    scale=stat2[:, 5:6], bias=stat2[:, 6:7]),
                    waits=[("V", v6_of2[t])] + list(xn_free[i2]), inc="S")
                xf_free[i] = (("S", s_norm),)
                norm_of2[t] = s_norm
                prev_norm2[0] = s_norm

            emit_ln2(0)
            emit_norm2(0)
            for t in range(C.TSO):
                i2 = 0
                if t + 1 < C.TSO:
                    emit_ln2(t + 1)
                s_norm = norm_of2[t]
                last_ev = 0
                for g in range(C.DC // 4):
                    b = g % 2
                    t_tr = None
                    for k in range(4):
                        c = g * 4 + k
                        t_tr = P.op("tensor", lambda e, i2=i2, c=c, b=b, k=k:
                                    e.matmul(
                            trps2[b][:, (c % 4) * 128:(c % 4) * 128 + 128],
                            xn2[i2][:, c * 128:(c + 1) * 128], ident_bf[:],
                            start=(k == 0), stop=(k == 3), skip_group_check=True),
                            waits=[("S", s_norm)]
                            + (list(trps_free[b]) if k == 0 else []),
                            inc="T" if k == 3 else None)
                    ev = P.op("vector", lambda e, t=t, g=g, b=b: e.tensor_copy(
                        xnT[:].rearrange("p (c n) -> p c n", c=C.DC)
                        [:, g * 4:(g + 1) * 4, t * 128:(t + 1) * 128],
                        trps2[b][:].rearrange("p (c n) -> p c n", c=4)),
                        waits=[("T", t_tr)], inc="V")
                    trps_free[b] = (("V", ev),)
                    last_ev = ev
                xn_free[i2] = (("V", last_ev),)
                if t + 1 < C.TSO:
                    emit_norm2(t + 1)
            xnT_done_v = P.c["V"]

            wcol_free = [(), ()]
            qct_free = [(), ()]
            ssq_last_mm = 0
            for qrc in range(C.QRC):
                wb = qrc % 2
                qw_w = P.op("gpsimd", lambda e, qrc=qrc, wb=wb: e.dma_start(
                    out=wcol[wb][:].rearrange("p (c m) -> p c m", c=C.DC),
                    in_=w_qa[:, qrc * 128:(qrc + 1) * 128].rearrange(
                        "(c p) m -> p c m", p=128)),
                    waits=wcol_free[wb], inc=f"qg{4 + wb}", incn=16)
                for qn in range(C.NQC):
                    pb = qn % 2
                    t_q = None
                    for c in range(C.DC):
                        t_q = P.op("tensor", lambda e, c=c, qn=qn, wb=wb, pb=pb:
                                   e.matmul(
                            qct_ps[pb][:],
                            wcol[wb][:, c * 128:(c + 1) * 128],
                            xnT[:, c * C.T_own + qn * 512:
                                c * C.T_own + qn * 512 + 512],
                            start=(c == 0), stop=(c == C.DC - 1)),
                            waits=([(f"qg{4 + wb}", qw_w), ("V", xnT_done_v)]
                                   + list(qct_free[pb])) if c == 0 else [],
                            inc="T" if c == C.DC - 1 else None)
                    s_sq = P.op("scalar", lambda e, pb=pb: e.activation(
                        qsq[:], qct_ps[pb][:], AF.Square),
                        waits=[("T", t_q)], inc="S")
                    v_raw = P.op("vector", lambda e, qrc=qrc, qn=qn, pb=pb:
                                 e.tensor_scalar(
                        q_cnT[:, qrc * C.T_own + qn * 512:
                              qrc * C.T_own + qn * 512 + 512],
                        qct_ps[pb][:], g_qaT[:, qrc:qrc + 1], None, ALU.mult),
                        waits=[("S", s_sq), ("qc1", qx_cst)], inc="V")
                    qct_free[pb] = (("V", v_raw),)
                    ssq_last_mm = P.op("tensor", lambda e, qrc=qrc, qn=qn:
                                       e.matmul(
                        ssq_ps[0:65, qn * 512:qn * 512 + 512],
                        ones_bf[:, 0:65], qsq[:],
                        start=(qrc == 0), stop=(qrc == C.QRC - 1),
                        skip_group_check=True),
                        waits=[("S", s_sq)], inc="T")
                wcol_free[wb] = (("T", ssq_last_mm),)
            s_r = P.op("scalar", lambda e: e.activation(
                rrow[:], ssq_ps[0:1, :], AF.Sqrt, scale=1.0 / C.QR),
                waits=[("T", ssq_last_mm)], inc="S")
            P.op("vector", lambda e: e.tensor_scalar(
                rrow[:], rrow[:], C.EPS_RMS, None, ALU.add),
                waits=[("S", s_r)], inc="V")
            v_rr = P.op("vector", lambda e: e.reciprocal(rrow[:], rrow[:]), inc="V")
            t_rb = None
            for (noff, nlen) in _nchunks(C.T_own):
                t_rb = P.op("tensor", lambda e, noff=noff, nlen=nlen: e.matmul(
                    rB_ps[:, noff:noff + nlen], ones_f32[0:1, :],
                    rrow[:, noff:noff + nlen], start=True, stop=True),
                    waits=[("V", v_rr)], inc="T")
            P.op("vector", lambda e: e.tensor_copy(rstdB[:], rB_ps[:]),
                 waits=[("T", t_rb)], inc="V")

            h1_free = [(), ()]
            for rc in range(C.RC):
                wb = rc % 2
                qw_w = P.op("gpsimd", lambda e, rc=rc, wb=wb: e.dma_start(
                    out=wcol[wb][:].rearrange("p (c m) -> p c m", c=C.DC),
                    in_=V[:, rc * 128:(rc + 1) * 128].rearrange(
                        "(c p) m -> p c m", p=128)),
                    waits=wcol_free[wb], inc=f"qg{4 + wb}", incn=16)
                for tn in range(C.NQC):
                    pb = tn % 2
                    t_h = None
                    for c in range(C.DC):
                        t_h = P.op("tensor", lambda e, c=c, tn=tn, wb=wb, pb=pb:
                                   e.matmul(
                            qct_ps[pb][:],
                            wcol[wb][:, c * 128:(c + 1) * 128],
                            xnT[:, c * C.T_own + tn * 512:
                                c * C.T_own + tn * 512 + 512],
                            start=(c == 0), stop=(c == C.DC - 1)),
                            waits=([(f"qg{4 + wb}", qw_w)] + list(h1_free[pb]))
                            if c == 0 else [],
                            inc="T" if c == C.DC - 1 else None)
                    v_h = P.op("vector", lambda e, rc=rc, tn=tn, pb=pb:
                               e.tensor_scalar(
                        h1sT[:, rc * C.T_own + tn * 512:
                             rc * C.T_own + tn * 512 + 512],
                        qct_ps[pb][:], SvT[:, rc:rc + 1], None, ALU.mult),
                        waits=[("T", t_h)], inc="V")
                    h1_free[pb] = (("V", v_h),)
                wcol_free[wb] = (("T", P.c["T"]),)
            P.barrier()

        if phases < 3:
            ph123.close()
            _early_exit(P, nc, out, const_sb)
            return _finish(nc, top, P)
        oT = E(nc.sbuf_tensor("oT", [128, C.NH * C.T_own], BF16, side="left"))
        with ExitStack() as ph3:
            E3 = ph3.enter_context
            rope2 = E3(nc.sbuf_tensor("rope2", [64, C.T_batch], BF16,
                                      side="right"))
            w_qb_bf = [E3(nc.sbuf_tensor(f"wqb{i}", [128, C.QRC * C.HW], BF16,
                                         side="right")) for i in range(2)]
            w_kvb_bf = [E3(nc.sbuf_tensor(f"wkvb{i}", [128, C.KVC * C.KW], BF16,
                                          side="right")) for i in range(2)]
            knopeT = E3(nc.sbuf_tensor("knopeT", [128, C.T_batch], BF16,
                                       side="right"))
            V1h = E3(nc.sbuf_tensor("V1h", [128, C.T_batch], BF16, side="right"))
            PT = [E3(nc.sbuf_tensor(f"PT{i}", [128, 512], BF16, side="right"))
                  for i in range(4)]
            qnopeT = E3(nc.sbuf_tensor("qnopeT", [128, C.T_own], BF16, side="right"))
            qropeM = E3(nc.sbuf_tensor("qropeM", [65, C.T_own], BF16, side="right"))
            qrtmp = E3(nc.sbuf_tensor("qrtmp", [64, C.T_own], BF16, side="right"))
            sq2 = E3(nc.sbuf_tensor("sq2", [128, max(1024, C.T_batch // 2)], BF16,
                                    side="right"))
            mrow = E3(nc.sbuf_tensor("mrow", [1, 16], F32, side="right"))
            bias_col = E3(nc.sbuf_tensor("bias_col", [128, 1], F32, side="right"))
            lrec_bf = E3(nc.sbuf_tensor("lrec_bf", [1, 512], BF16, side="right"))
            recipB = E3(nc.sbuf_tensor("recipB", [128, 512], BF16, side="right"))
            kx_ps = [E3(nc.psum_tensor(f"kxps{i}", [128, 512], F32))
                     for i in range(2)]
            s_ps = [E3(nc.psum_tensor(f"sps{i}", [128, 512], F32))
                    for i in range(4)]
            av_ps = E3(nc.psum_tensor("avps", [128, 512], F32))
            l_ps = E3(nc.psum_tensor("lps", [65, 512], F32))

            wqb_free = [(), ()]
            wkvb_free = [(), ()]
            kx_free = [(), ()]
            v_r2 = P.op("vector", lambda e: e.tensor_tensor(
                rope2[:], ropeP[0:64, :], ropeP[0:64, :], ALU.mult), inc="V")
            s_free = [(), (), (), ()]
            qsq3_free = ()
            sq2_slot_free = [(), ()]
            qsq3 = E3(nc.sbuf_tensor("qsq3", [128, 512], BF16, side="right"))
            PT_free = [(), (), (), ()]
            av_free = ()
            l_free = ()
            pending_fin = [None]

            def flush_fin(bank=0):
                if pending_fin[0] is not None:
                    pending_fin[0](bank)
                    pending_fin[0] = None
            knopeT_free = ()
            V1h_free = ()
            q_free = ()

            qw_head = [0, 0]
            for h in range(C.NH):
                wb = h % 2
                qw_head[wb] = P.op("gpsimd", lambda e, h=h, wb=wb: e.dma_start(
                    out=w_qb_bf[wb][:].rearrange("p (c m) -> p c m", c=C.QRC),
                    in_=w_qb[:, h * C.HW:(h + 1) * C.HW].rearrange(
                        "(c p) m -> p c m", p=128)),
                    waits=wqb_free[wb], inc=f"qg{wb}", incn=16)
                qw_kvb = P.op("gpsimd", lambda e, h=h, wb=wb: e.dma_start(
                    out=w_kvb_bf[wb][:].rearrange("p (c m) -> p c m", c=C.KVC),
                    in_=w_kvb[:, h * C.KW:(h + 1) * C.KW].rearrange(
                        "(c p) m -> p c m", p=128)),
                    waits=wkvb_free[wb], inc=f"qg{2 + wb}", incn=16)

                NKN = C.T_batch // 512
                kn_ev = [0] * NKN
                for n in range(NKN):
                    pb = n % 2
                    t_k = None
                    for c in range(C.KVC):
                        t_k = P.op("tensor", lambda e, c=c, n=n, wb=wb, pb=pb:
                                   e.matmul(
                            kx_ps[pb][:],
                            w_kvb_bf[wb][:, c * C.KW:c * C.KW + C.ND],
                            k_cnT[:, c * C.T_batch + n * 512:
                                  c * C.T_batch + n * 512 + 512],
                            start=(c == 0), stop=(c == C.KVC - 1)),
                            waits=([(f"qg{2 + wb}", qw_kvb)] + list(kx_free[pb])
                                   + (list(knopeT_free) if n == 0 else []))
                            if c == 0 else [],
                            inc="T" if c == C.KVC - 1 else None)
                    v_ev = P.op("vector", lambda e, n=n, pb=pb: e.tensor_copy(
                        knopeT[:, n * 512:(n + 1) * 512], kx_ps[pb][:]),
                        waits=[("T", t_k)], inc="V")
                    kx_free[pb] = (("V", v_ev),)
                    kn_ev[n] = v_ev
                    if n == 0:
                        flush_fin(bank=1)

                last_vev = 0
                for grp in range(C.TSB // 4):
                    pb = grp % 2
                    t_v = None
                    for slot in range(4):
                        ts = grp * 4 + slot
                        for c in range(C.KVC):
                            t_v = P.op("tensor", lambda e, c=c, ts=ts, slot=slot,
                                 wb=wb, pb=pb: e.matmul(
                                kx_ps[pb][:, slot * 128:(slot + 1) * 128],
                                k_cnT[:, c * C.T_batch + ts * 128:
                                      c * C.T_batch + ts * 128 + 128],
                                w_kvb_bf[wb][:, c * C.KW + C.ND:
                                             c * C.KW + C.ND + C.VD],
                                start=(slot == 0 and c == 0),
                                stop=(slot == 3 and c == C.KVC - 1),
                                skip_group_check=True),
                                waits=(list(kx_free[pb])
                                       + (list(V1h_free) if grp == 0 else []))
                                if (c == 0 and slot == 0) else [],
                                inc="T" if (slot == 3 and c == C.KVC - 1)
                                else None)
                    last_vev = P.op("vector", lambda e, grp=grp, pb=pb:
                                    e.tensor_copy(
                        V1h[:, grp * 512:(grp + 1) * 512], kx_ps[pb][:]),
                        waits=[("T", t_v)], inc="V")
                    kx_free[pb] = (("V", last_vev),)
                NKB = NKN // 2 if NKN % 2 == 0 else NKN
                wide = (NKN % 2 == 0)
                kw = 1024 if wide else 512
                s_k2_of = {}

                def emit_k2(nb, kn_ev=kn_ev):
                    sl = nb % 2
                    s_k2_of[nb] = P.op(
                        "scalar", lambda e, nb=nb, sl=sl, kw=kw: e.activation(
                            sq2[:, sl * kw:(sl + 1) * kw],
                            knopeT[:, nb * kw:(nb + 1) * kw], AF.Square),
                        waits=[("V", kn_ev[min((nb + 1) * (2 if wide else 1),
                                               NKN) - 1])]
                        + list(sq2_slot_free[sl]), inc="S")

                emit_k2(0)
                if NKB > 1:
                    emit_k2(1)
                t_qn = [None, None]
                t_qr = [None, None]
                for qn in range(C.NQC):
                    for c in range(C.QRC):
                        t_qn[qn] = P.op("tensor", lambda e, c=c, qn=qn, wb=wb:
                                        e.matmul(
                            s_ps[qn][:],
                            w_qb_bf[wb][:, c * C.HW:c * C.HW + C.ND],
                            q_cnT[:, c * C.T_own + qn * 512:
                                  c * C.T_own + qn * 512 + 512],
                            start=(c == 0), stop=(c == C.QRC - 1)),
                            waits=([(f"qg{wb}", qw_head[wb])] + list(s_free[qn])
                                   + list(q_free)) if c == 0 else [],
                            inc="T" if c == C.QRC - 1 else None)
                        t_qr[qn] = P.op("tensor", lambda e, c=c, qn=qn, wb=wb:
                                        e.matmul(
                            s_ps[2 + qn][0:64, :],
                            w_qb_bf[wb][:, c * C.HW + C.ND:c * C.HW + C.HW],
                            q_cnT[:, c * C.T_own + qn * 512:
                                  c * C.T_own + qn * 512 + 512],
                            start=(c == 0), stop=(c == C.QRC - 1)),
                            waits=list(s_free[2 + qn]) if c == 0 else [],
                            inc="T" if c == C.QRC - 1 else None)
                ev_qn = [None, None]
                ev_qr = [None, None]
                for qn in range(C.NQC):
                    ev_qn[qn] = P.op("vector", lambda e, qn=qn: e.tensor_tensor(
                        qnopeT[:, qn * 512:qn * 512 + 512], s_ps[qn][:],
                        rstdB[:, qn * 512:qn * 512 + 512], ALU.mult),
                        waits=[("T", t_qn[qn])], inc="V")
                    ev_qr[qn] = P.op("vector", lambda e, qn=qn: e.tensor_tensor(
                        qrtmp[0:64, qn * 512:qn * 512 + 512],
                        s_ps[2 + qn][0:64, :],
                        rstdB[0:64, qn * 512:qn * 512 + 512], ALU.mult),
                        waits=[("T", t_qr[qn])], inc="V")
                q_nope_v = max(x for x in [ev_qn[0], ev_qn[C.NQC - 1]])
                for qn in range(C.NQC):
                    v_s1 = P.op("scalar", lambda e, qn=qn: e.activation(
                        qsq3[:, 0:512], qnopeT[:, qn * 512:qn * 512 + 512],
                        AF.Square),
                        waits=[("V", ev_qn[qn])] + list(qsq3_free), inc="S")
                    t_n1 = P.op("tensor", lambda e, qn=qn: e.matmul(
                        s_ps[qn][0:65, :], ones_bf[:, 0:65], qsq3[:, 0:512],
                        start=True, stop=False, skip_group_check=True),
                        waits=[("S", v_s1), ("V", ev_qn[qn])]
                        + list(s_free[qn]), inc="T")
                    v_s2 = P.op("scalar", lambda e, qn=qn: e.activation(
                        qsq3[0:64, 0:512], qrtmp[0:64, qn * 512:qn * 512 + 512],
                        AF.Square),
                        waits=[("V", ev_qr[qn]), ("T", t_n1)], inc="S")
                    t_n = P.op("tensor", lambda e, qn=qn: e.matmul(
                        s_ps[qn][0:65, :], ones_bf[0:64, 0:65],
                        qsq3[0:64, 0:512],
                        start=False, stop=True, skip_group_check=True),
                        waits=[("S", v_s2)], inc="T")
                    v_m = P.op("vector", lambda e, qn=qn: e.tensor_scalar(
                        qropeM[64:65, qn * 512:qn * 512 + 512],
                        s_ps[qn][64:65, :],
                        -0.5, None, ALU.mult), waits=[("T", t_n)], inc="V")
                    s_free[qn] = (("V", v_m),)
                    qsq3_free = (("T", t_n),)
                t_pm = [None, None]
                ev_pm = [None, None]
                for qn in range(C.NQC):
                    t_pm[qn] = P.op("tensor", lambda e, qn=qn: e.matmul(
                        s_ps[2 + qn][0:64, :], perm_bf[:],
                        qrtmp[0:64, qn * 512:qn * 512 + 512],
                        start=True, stop=True),
                        waits=[("V", ev_qr[qn])], inc="T")
                    ev_pm[qn] = P.op("vector", lambda e, qn=qn: e.tensor_copy(
                        qropeM[0:64, qn * 512:qn * 512 + 512],
                        s_ps[2 + qn][0:64, :]),
                        waits=[("T", t_pm[qn])], inc="V")
                    s_free[2 + qn] = (("V", ev_pm[qn]),)
                v_c2 = P.op("vector", lambda e: e.tensor_tensor(
                    qropeM[0:64, :], qropeM[0:64, :], sinQ[:], ALU.mult),
                    inc="V")
                v_c1 = P.op("vector", lambda e: e.tensor_tensor(
                    qrtmp[:], qrtmp[:], cosQ[:], ALU.mult),
                    waits=[("S", P.c["S"])], inc="V")
                v_rot = P.op("vector", lambda e: e.tensor_tensor(
                    qropeM[0:64, :], qropeM[0:64, :], qrtmp[:], ALU.add),
                    inc="V")
                for nb in range(NKB):
                    sl = nb % 2
                    s_k2 = s_k2_of[nb]
                    t_kn = None
                    for h2 in range(2 if wide else 1):
                        b2 = (nb * (2 if wide else 1) + h2) % 2
                        P.op("tensor", lambda e, nb=nb, sl=sl, h2=h2, kw=kw,
                             b2=b2: e.matmul(
                            s_ps[b2][0:65, :], ones_bf[:, 0:65],
                            sq2[:, sl * kw + h2 * 512:sl * kw + h2 * 512 + 512],
                            start=True, stop=False, skip_group_check=True),
                            waits=[("S", s_k2)] + list(s_free[b2]))
                        t_kn = P.op("tensor", lambda e, nb=nb, h2=h2, kw=kw,
                                    b2=b2: e.matmul(
                            s_ps[b2][0:65, :], ones_bf[0:64, 0:65],
                            rope2[:, nb * kw + h2 * 512:
                                  nb * kw + h2 * 512 + 512],
                            start=False, stop=True, skip_group_check=True),
                            inc="T")
                        v_mx = P.op("vector", lambda e, nb=nb, h2=h2, b2=b2:
                                    e.reduce_max(
                            mrow[:, nb * (2 if wide else 1) + h2:
                                 nb * (2 if wide else 1) + h2 + 1],
                            s_ps[b2][0:1, :], axis=AX.X),
                            waits=[("T", t_kn)], inc="V")
                        s_free[b2] = (("V", v_mx),)
                    sq2_slot_free[sl] = (("T", t_kn),)
                    if nb + 2 < NKB:
                        emit_k2(nb + 2)
                v_mx2 = P.op("vector", lambda e: e.reduce_max(
                    mrow[:, 15:16], mrow[:, 0:NKN], axis=AX.X), inc="V")
                v_mx3 = P.op("vector", lambda e: e.tensor_scalar(
                    mrow[:, 14:15], mrow[:, 15:16],
                    -0.5 * C.SCALE, None, ALU.mult), inc="V")
                t_bc = P.op("tensor", lambda e: e.matmul(
                    s_ps[1][:, 0:1], ones_f32[0:1, :], mrow[:, 14:15],
                    start=True, stop=True, skip_group_check=True),
                    waits=[("V", v_mx3)] + list(s_free[1]), inc="T")
                v_bc = P.op("vector", lambda e: e.tensor_copy(
                    bias_col[:], s_ps[1][:, 0:1]), waits=[("T", t_bc)], inc="V")
                s_free[1] = (("V", v_bc),)
                q_ready_v = P.c["V"]

                for qc in range(C.NQC):
                    exp_done = [0] * C.KS
                    t_avl = [0] * C.KS

                    def emit_av(j, qc=qc):
                        sb = j % 4
                        P.op("tensor", lambda e, j=j, sb=sb: e.matmul(
                            l_ps[0:65, :], ones_bf[:, 0:65], PT[sb][:],
                            start=(j == 0), stop=(j == C.KS - 1),
                            skip_group_check=True),
                            waits=[("S", exp_done[j])]
                            + (list(l_free) if j == 0 else []))
                        t_avl[j] = P.op("tensor", lambda e, j=j, sb=sb, qc=qc:
                                        e.matmul(
                            av_ps[:],
                            V1h[:, j * 128:(j + 1) * 128],
                            PT[sb][:],
                            start=(j == 0), stop=(j == C.KS - 1),
                            skip_group_check=True),
                            waits=([("V", last_vev)] + list(av_free))
                            if j == 0 else [],
                            inc="T")
                        PT_free[sb] = (("T", t_avl[j]),)

                    for ks in range(C.KS):
                        sb = ks % 4
                        P.op("tensor", lambda e, ks=ks, sb=sb, qc=qc: e.matmul(
                            s_ps[sb][:],
                            knopeT[:, ks * 128:(ks + 1) * 128],
                            qnopeT[:, qc * 512:qc * 512 + 512],
                            start=True, stop=False),
                            waits=(([("V", q_nope_v)] if ks == 0 else [])
                                   + (list(s_free[sb]) if ks < 4 else [])))
                        t_s = P.op("tensor", lambda e, ks=ks, sb=sb, qc=qc:
                                   e.matmul(
                            s_ps[sb][:],
                            ropeP[:, ks * 128:(ks + 1) * 128],
                            qropeM[:, qc * 512:qc * 512 + 512],
                            start=False, stop=True),
                            waits=[("V", q_ready_v)] if ks == 0 else [],
                            inc="T")
                        exp_done[ks] = P.op("scalar", lambda e, sb=sb:
                                            e.activation(
                            PT[sb][:], s_ps[sb][:], AF.Exp,
                            scale=C.SCALE, bias=bias_col[:]),
                            waits=[("T", t_s)]
                            + ([("V", v_bc)] if ks == 0 else [])
                            + (list(PT_free[sb]) if ks < 4 else []), inc="S")
                        s_free[sb] = (("S", exp_done[ks]),)
                        if ks == 2:
                            flush_fin()
                        if ks >= 3:
                            emit_av(ks - 3)
                    emit_av(C.KS - 3)
                    emit_av(C.KS - 2)
                    emit_av(C.KS - 1)
                    t_att = P.c["T"]

                    def _fin(bank, h=h, qc=qc, t_att=t_att):
                        nonlocal av_free, l_free, kx_free
                        def _recip(e):
                            with nc.allow_low_precision(
                                    "1/l in bf16: softmax scale factor"):
                                return e.reciprocal(lrec_bf[:], l_ps[0:1, :])
                        v_lr = P.op("vector", _recip,
                                    waits=[("T", t_att)], inc="V")
                        if bank == 0:
                            bps, bfree = kx_ps[1], kx_free[1]
                        else:
                            bps, bfree = s_ps[1], s_free[1]
                        t_rb = P.op("tensor", lambda e, bps=bps: e.matmul(
                            bps[:], ones_bf[0:1, :], lrec_bf[:],
                            start=True, stop=True, skip_group_check=True),
                            waits=[("V", v_lr)] + list(bfree), inc="T")
                        v_rb = P.op("vector", lambda e, bps=bps: e.tensor_copy(
                            recipB[:], bps[:]), waits=[("T", t_rb)], inc="V")
                        if bank == 0:
                            kx_free[1] = (("V", v_rb),)
                        else:
                            s_free[1] = (("V", v_rb),)
                        v_o = P.op("vector", lambda e, h=h, qc=qc:
                                   e.tensor_tensor(
                            oT[:, h * C.T_own + qc * 512:
                               h * C.T_own + qc * 512 + 512],
                            av_ps[:], recipB[:], ALU.mult), inc="V")
                        av_free = (("V", v_o),)
                        l_free = (("V", v_lr),)
                    pending_fin[0] = _fin
                wqb_free[wb] = (("T", P.c["T"]),)
                wkvb_free[wb] = (("T", P.c["T"]),)
                knopeT_free = (("T", P.c["T"]),)
                V1h_free = (("T", P.c["T"]),)
                q_free = (("T", P.c["T"]),)
            flush_fin()
            P.barrier()
        ph123.close()

        if phases < 4:
            _early_exit(P, nc, out, const_sb)
            return _finish(nc, top, P)
        with ExitStack() as ph4:
            E4 = ph4.enter_context
            wo_buf = [E4(nc.sbuf_tensor(f"wo{i}", [128, C.NH * 512], BF16,
                                        side="right")) for i in range(2)]
            u_tmp = E4(nc.sbuf_tensor("u_tmp", [128, 4 * C.RANK], BF16,
                                      side="right"))
            uT = [E4(nc.sbuf_tensor(f"uT{i}", [128, C.RC * 512], BF16,
                                    side="right")) for i in range(2)]
            xres = [E4(nc.sbuf_tensor(f"xres{i}", [128, 512], F32, side="right"))
                    for i in range(2)]
            stage = [E4(nc.sbuf_tensor(f"stage{i}", [128, 512], F32, side="right"))
                     for i in range(2)]
            out_ps = [E4(nc.psum_tensor(f"outps{i}", [128, 512], F32))
                      for i in range(2)]
            trps4 = [E4(nc.psum_tensor(f"p4trps{i}", [128, 512], F32))
                    for i in range(2)]

            wo_free = [(), ()]
            uT_free = [(), ()]
            utmp_free = ()
            xres_free = [(), ()]
            stage_free = [(), ()]
            outps_free = [(), ()]
            trps_free = [(), ()]
            for j in range(C.NJ):
                wb = j % 2
                qw_wo = P.op("gpsimd", lambda e, j=j, wb=wb: e.dma_start(
                    out=wo_buf[wb][:].rearrange("p (c m) -> p c m", c=C.NH),
                    in_=w_o[:, j * 512:(j + 1) * 512].rearrange(
                        "(c p) m -> p c m", p=128)),
                    waits=wo_free[wb], inc=f"qg{wb}", incn=16)
                qw_u = P.op("gpsimd", lambda e, j=j: e.dma_start(
                    out=u_tmp[:].rearrange("p (c m) -> p c m", c=4),
                    in_=U[j * 512:(j + 1) * 512, :].rearrange(
                        "(c p) m -> p c m", p=128)),
                    waits=utmp_free, inc="qc0", incn=16)
                n_tr = 0
                uev = []
                for rb in range(C.RC):
                    pb = rb % 2
                    t_tr = None
                    for c4 in range(4):
                        t_tr = P.op("tensor", lambda e, rb=rb, c4=c4, pb=pb:
                                    e.matmul(
                            trps4[pb][:, c4 * 128:(c4 + 1) * 128],
                            u_tmp[:, c4 * C.RANK + rb * 128:
                                  c4 * C.RANK + rb * 128 + 128],
                            ident_bf[:], start=(c4 == 0), stop=(c4 == 3),
                            skip_group_check=True),
                            waits=([("qc0", qw_u)] + list(trps_free[pb])
                                   + (list(uT_free[wb]) if rb == 0 else []))
                            if c4 == 0 else [],
                            inc="T" if c4 == 3 else None)
                    ev = P.op("vector", lambda e, rb=rb, pb=pb, wb=wb:
                              e.tensor_copy(
                        uT[wb][:, rb * 512:(rb + 1) * 512], trps4[pb][:]),
                        waits=[("T", t_tr)], inc="V")
                    trps_free[pb] = (("V", ev),)
                    uev.append(ev)
                uT_ready = uev[-1]
                utmp_free = (("T", P.c["T"]),)
                for tc in range(C.TSO):
                    pb = tc % 2
                    qx_x = P.op("sync", lambda e, j=j, tc=tc, pb=pb: e.dma_start(
                        out=xres[pb][:],
                        in_=x[tc * 128:(tc + 1) * 128, j * 512:(j + 1) * 512]),
                        waits=xres_free[pb], inc=f"qs{2 + pb}", incn=16)
                    for c in range(C.NH):
                        P.op("tensor", lambda e, c=c, j=j, tc=tc, wb=wb, pb=pb:
                             e.matmul(
                            out_ps[pb][:],
                            oT[:, c * C.T_own + tc * 128:
                               c * C.T_own + tc * 128 + 128],
                            wo_buf[wb][:, c * 512:(c + 1) * 512],
                            start=(c == 0), stop=False),
                            waits=([(f"qg{wb}", qw_wo)] + list(outps_free[pb]))
                            if c == 0 else [])
                    t_o = None
                    for rc in range(C.RC):
                        t_o = P.op("tensor", lambda e, rc=rc, tc=tc, wb=wb, pb=pb:
                             e.matmul(
                            out_ps[pb][:],
                            h1sT[:, rc * C.T_own + tc * 128:
                                 rc * C.T_own + tc * 128 + 128],
                            uT[wb][:, rc * 512:(rc + 1) * 512],
                            start=False, stop=(rc == C.RC - 1)),
                            waits=[("V", uT_ready)] if rc == 0 else [],
                            inc="T" if rc == C.RC - 1 else None)
                    v_st = P.op("vector", lambda e, pb=pb: e.tensor_tensor(
                        stage[pb][:], out_ps[pb][:], xres[pb][:], ALU.add),
                        waits=[("T", t_o), (f"qs{2 + pb}", qx_x)] + list(stage_free[pb]),
                        inc="V")
                    outps_free[pb] = (("V", v_st),)
                    xres_free[pb] = (("V", v_st),)
                    qo_st = P.op("sync", lambda e, j=j, tc=tc, pb=pb: e.dma_start(
                        out=out[tc * 128:(tc + 1) * 128, j * 512:(j + 1) * 512],
                        in_=stage[pb][:]),
                        waits=[("V", v_st)], inc=f"qo{pb}", incn=16)
                    stage_free[pb] = ((f"qo{pb}", qo_st),)
                uT_free[wb] = (("T", P.c["T"]),)
                wo_free[wb] = (("T", P.c["T"]),)

            P.wait("sync", ("qo0", P.c["qo0"]), ("qo1", P.c["qo1"]))

        return _finish(nc, top, P)


def _early_exit(P, nc, out, const_sb):
    P.barrier()
    qo = P.op("sync", lambda e: e.dma_start(
        out=out[0:128, 0:8], in_=const_sb[:, 0:8]), inc="qo0", incn=16)
    P.wait("sync", ("qo0", qo))


def _finish(nc, top, P):
    with nc.Block() as block:
        block.sync(lambda e: [f(e) for f in P.sched["sync"]])
        block.gpsimd(lambda e: [f(e) for f in P.sched["gpsimd"]])
        block.vector(lambda e: [f(e) for f in P.sched["vector"]])
        block.scalar(lambda e: [f(e) for f in P.sched["scalar"]])
        block.tensor(lambda e: [f(e) for f in P.sched["tensor"]])
    top.close()
    return nc


def make_tables(cfg: Cfg):
    t = np.arange(cfg.T_batch, dtype=np.float64)
    inv = 1.0 / (10000.0 ** (np.arange(0, cfg.RD, 2, dtype=np.float64) / cfg.RD))
    freqs = np.outer(t, inv)
    emb = np.concatenate([freqs, freqs], axis=-1)
    cos = np.cos(emb).T.astype(np.float32)
    sin = np.sin(emb).T.astype(np.float32)
    return np.ascontiguousarray(cos), np.ascontiguousarray(sin)


def make_consts():
    ident = np.eye(128, dtype=np.float32)
    onesm = np.ones((128, 128), dtype=np.float32)
    perm = np.zeros((64, 64), dtype=np.float32)
    for i in range(32):
        perm[i + 32, i] = -1.0
        perm[i, i + 32] = 1.0
    return ident, onesm, perm


def make_in_maps(cfg, weights, x_full, n_rot):
    cos, sin = make_tables(cfg)
    ident, onesm, perm = make_consts()
    maps = []
    for r in n_rot:
        xr = np.ascontiguousarray(np.roll(x_full, -r, axis=0))
        cr = np.ascontiguousarray(np.roll(cos, -r, axis=1))
        sr = np.ascontiguousarray(np.roll(sin, -r, axis=1))
        m = dict(weights)
        m.update(x=xr, cosT=cr, sinT=sr, ident=ident, ones=onesm, perm=perm)
        maps.append(m)
    return maps


def kernel(L0_attn_hidden_states, L0_attn_w_qa, L0_attn_g_qa, L0_attn_w_qb,
           L0_attn_w_kva, L0_attn_g_kva, L0_attn_w_kvb, L0_attn_w_o,
           L0_ffn_U, L0_ffn_S, L0_ffn_V):
    cfg = FULL
    nc = build_program(cfg)
    weights = {
        "w_qa": np.ascontiguousarray(L0_attn_w_qa, dtype=np.float32),
        "g_qa": np.ascontiguousarray(L0_attn_g_qa, dtype=np.float32),
        "w_qb": np.ascontiguousarray(L0_attn_w_qb, dtype=np.float32),
        "w_kva": np.ascontiguousarray(L0_attn_w_kva, dtype=np.float32),
        "g_kva": np.ascontiguousarray(L0_attn_g_kva, dtype=np.float32),
        "w_kvb": np.ascontiguousarray(L0_attn_w_kvb, dtype=np.float32),
        "w_o": np.ascontiguousarray(L0_attn_w_o, dtype=np.float32),
        "U": np.ascontiguousarray(L0_ffn_U, dtype=np.float32),
        "Sv": np.ascontiguousarray(L0_ffn_S, dtype=np.float32),
        "V": np.ascontiguousarray(L0_ffn_V, dtype=np.float32),
    }
    x = np.asarray(L0_attn_hidden_states, dtype=np.float32)
    in_maps = []
    for core in range(8):
        b, jq = core // 4, core % 4
        in_maps.extend(make_in_maps(cfg, weights, x[b], [jq * cfg.T_own]))
    res = run_bass_kernel_spmd(nc, in_maps, core_ids=list(range(8)))
    outp = np.empty((2, 4096, 4096), dtype=np.float32)
    for core in range(8):
        b, jq = core // 4, core % 4
        outp[b, jq * cfg.T_own:(jq + 1) * cfg.T_own, :] = res.results[core]["out"]
    return outp


# version 28
# speedup vs baseline: 1.2214x; 1.2214x over previous
import math
import numpy as np
from contextlib import ExitStack

import concourse.bass as bass
import concourse.mybir as mybir
from concourse.bass_utils import run_bass_kernel_spmd

F32 = mybir.dt.float32
BF16 = mybir.dt.bfloat16
AF = mybir.ActivationFunctionType
ALU = mybir.AluOpType
AX = mybir.AxisListType


class Cfg:
    def __init__(self, D=4096, QR=1536, KVR=512, NH=32, RANK=1024,
                 T_own=1024, T_batch=4096):
        self.D, self.QR, self.KVR, self.NH, self.RANK = D, QR, KVR, NH, RANK
        self.T_own, self.T_batch = T_own, T_batch
        self.RD, self.ND, self.VD = 64, 128, 128
        self.QK = self.ND + self.RD
        self.DC = D // 128
        self.QRC = QR // 128
        self.KVC = KVR // 128
        self.RC = RANK // 128
        self.TSB = T_batch // 128
        self.TSO = T_own // 128
        self.NQC = T_own // 512
        self.KS = T_batch // 128
        self.NJ = D // 512
        self.KVW = KVR + self.RD
        self.HW = self.ND + self.RD
        self.KW = self.ND + self.VD
        self.EPS_LN = 1e-5
        self.EPS_RMS = 1e-6
        self.SCALE = self.QK ** (-0.5)
        assert T_own % 512 == 0 and T_batch % 256 == 0


FULL = Cfg()


def _nchunks(total, step=512):
    out = []
    off = 0
    while off < total:
        out.append((off, min(step, total - off)))
        off += step
    return out


class Plan:

    ENGINES = ("sync", "gpsimd", "vector", "scalar", "tensor")

    def __init__(self, nc, ctx, sem_names):
        self.nc = nc
        self.sems = {}
        self.c = {}
        for s in sem_names:
            self.sems[s] = ctx.enter_context(nc.semaphore(s))
            self.c[s] = 0
        self.sched = {e: [] for e in self.ENGINES}

    OWN = {"vector": "V", "scalar": "S"}

    def op(self, eng, build, waits=(), inc=None, incn=1):
        own = self.OWN.get(eng)
        if own is not None:
            waits = tuple(waits) + ((own, self.c[own]),)
            if inc is None:
                inc = own
        waits = tuple((s, v) for (s, v) in waits if v > 0)
        sems = self.sems

        def run(e, waits=waits, build=build, inc=inc, incn=incn):
            for s, v in waits:
                e.wait_ge(sems[s], v)
            ins = build(e)
            if inc is not None:
                ins.then_inc(sems[inc], incn)
        self.sched[eng].append(run)
        if inc is not None:
            self.c[inc] += incn
            return self.c[inc]
        return None

    def wait(self, eng, *waits):
        waits = tuple((s, v) for (s, v) in waits if v > 0)
        if not waits:
            return
        sems = self.sems

        def run(e, waits=waits):
            for s, v in waits:
                e.wait_ge(sems[s], v)
        self.sched[eng].append(run)

    def barrier(self):
        snap = tuple((s, v) for s, v in self.c.items() if v > 0)
        for e in self.ENGINES:
            self.wait(e, *snap)


def build_program(cfg: Cfg, phases=4):
    nc = bass.Bass("TRN2", target_bir_lowering=False, debug=False)
    C = cfg
    DT = {}

    def din(name, shape):
        DT[name] = nc.dram_tensor(name, list(shape), F32, kind="ExternalInput").ap()
        return DT[name]

    x = din("x", (C.T_batch, C.D))
    w_qa = din("w_qa", (C.D, C.QR))
    g_qa = din("g_qa", (C.QR,))
    w_qb = din("w_qb", (C.QR, C.NH * C.HW))
    w_kva = din("w_kva", (C.D, C.KVW))
    g_kva = din("g_kva", (C.KVR,))
    w_kvb = din("w_kvb", (C.KVR, C.NH * C.KW))
    w_o = din("w_o", (C.NH * C.VD, C.D))
    U = din("U", (C.D, C.RANK))
    Sv = din("Sv", (C.RANK,))
    V = din("V", (C.D, C.RANK))
    cosT = din("cosT", (C.RD, C.T_batch))
    sinT = din("sinT", (C.RD, C.T_batch))
    ident = din("ident", (128, 128))
    ones = din("ones", (128, 128))
    perm = din("perm", (64, 64))
    out = nc.dram_tensor("out", [C.T_own, C.D], F32, kind="ExternalOutput").ap()

    with ExitStack() as top:
        E = top.enter_context
        P = Plan(nc, top, ["qs0", "qs1", "qs2", "qs3", "qg0", "qg1", "qg2", "qg3",
                           "qg4", "qg5", "qc0", "qc1", "qo0", "qo1",
                           "T", "V", "S"])

        ident_bf = E(nc.sbuf_tensor("ident_bf", [128, 128], BF16, side="left"))
        ones_bf = E(nc.sbuf_tensor("ones_bf", [128, 128], BF16, side="left"))
        ones_f32 = E(nc.sbuf_tensor("ones_f32", [128, 128], F32, side="left"))
        perm_bf = E(nc.sbuf_tensor("perm_bf", [64, 64], BF16, side="left"))
        h1sT = E(nc.sbuf_tensor("h1sT", [128, C.RC * C.T_own], BF16, side="left"))
        ph123 = ExitStack()
        E123 = ph123.enter_context
        cosQ = E123(nc.sbuf_tensor("cosQ", [64, C.T_own], BF16, side="right"))
        sinQ = E123(nc.sbuf_tensor("sinQ", [64, C.T_own], BF16, side="right"))
        ropeP = E123(nc.sbuf_tensor("ropeP", [65, C.T_batch], BF16, side="right"))
        k_cnT = E123(nc.sbuf_tensor("k_cnT", [128, C.KVC * C.T_batch], BF16,
                                    side="right"))

        def cast_dma(dst_ap, src_ap):
            return lambda e: e.dma_start(out=dst_ap, in_=src_ap)

        P.op("gpsimd", cast_dma(ident_bf[:], ident[:, :]), inc="qc0", incn=16)
        P.op("gpsimd", cast_dma(ones_bf[:], ones[:, :]), inc="qc0", incn=16)
        P.op("gpsimd", cast_dma(perm_bf[:], perm[:, :]), inc="qc0", incn=16)
        P.op("gpsimd", lambda e: e.dma_start(out=ones_f32[:], in_=ones[:, :]),
             inc="qc0", incn=16)
        P.op("vector", lambda e: e.memset(ropeP[64:65, :], 1.0), inc="V")
        const_vals = [C.EPS_LN, 1.0 / C.KVR, 1.0 / C.QR, C.SCALE, 1.0, 0.0,
                      1.0 / C.D, C.EPS_RMS, -1.0, -0.5]
        const_sb = E(nc.sbuf_tensor("const_sb", [128, len(const_vals)], F32,
                                    side="left"))
        cv = 0
        for i, val in enumerate(const_vals):
            cv = P.op("vector", lambda e, i=i, val=val: e.memset(
                const_sb[:, i:i + 1], val), inc="V")
            nc.const_aps.register(F32, val, const_sb[:, i:i + 1])
        P.wait("scalar", ("V", cv))
        P.wait("vector", ("V", cv))
        P.wait("tensor", ("V", cv))

        with ExitStack() as ph1:
            E1 = ph1.enter_context
            cosT_bf = E1(nc.sbuf_tensor("cosT_bf", [64, C.T_batch], BF16,
                                        side="right"))
            sinT_bf = E1(nc.sbuf_tensor("sinT_bf", [64, C.T_batch], BF16,
                                        side="right"))
            w_kva_bf = E1(nc.sbuf_tensor("w_kva_bf", [128, C.DC * C.KVW], BF16,
                                         side="right"))
            g_kvaB = E1(nc.sbuf_tensor("g_kvaB", [128, C.KVR], F32, side="right"))
            xf = [E1(nc.sbuf_tensor(f"xf{i}", [128, C.D], BF16, side="right"))
                  for i in range(4)]
            xn_bf = [E1(nc.sbuf_tensor(f"xn{i}", [128, C.D], BF16, side="right"))
                     for i in range(2)]
            xnT_tmp = [E1(nc.sbuf_tensor(f"xnTt{i}", [128, C.DC * 128], BF16,
                                         side="right")) for i in range(2)]
            kvf = E1(nc.sbuf_tensor("kvf", [128, C.KVW], F32, side="right"))
            kvn_bf = [E1(nc.sbuf_tensor(f"kvn_bf{j}", [128, C.KVC * 128 + 128],
                                        BF16, side="right")) for j in range(2)]
            kc_tmp = E1(nc.sbuf_tensor("kc_tmp", [128, C.KVR], BF16, side="right"))
            kropeT = E1(nc.sbuf_tensor("kropeT", [128, C.T_batch], BF16, side="right"))
            ropetmp = E1(nc.sbuf_tensor("ropetmp", [64, C.T_batch], BF16, side="right"))
            stat = E1(nc.sbuf_tensor("stat", [128, 16], F32, side="right"))
            bnst = E1(nc.sbuf_tensor("bnst", [128, 48], F32, side="right"))
            kvps = [E1(nc.psum_tensor(f"kvps{i}", [128, C.KVW], F32)) for i in range(2)]
            trps = [E1(nc.psum_tensor(f"trps{i}", [128, 512], F32)) for i in range(2)]
            kcn_ps = E1(nc.psum_tensor("kcnps", [128, C.KVC * 128], F32))
            krp_ps = E1(nc.psum_tensor("krpps", [128, 128], F32))

            P.op("gpsimd", cast_dma(cosT_bf[:], cosT[:, :]), inc="qc0", incn=16)
            P.op("gpsimd", cast_dma(sinT_bf[:], sinT[:, :]), inc="qc0", incn=16)
            P.op("gpsimd", cast_dma(
                w_kva_bf[:].rearrange("p (c m) -> p c m", c=C.DC),
                w_kva.rearrange("(c p) m -> p c m", p=128)), inc="qc0", incn=16)
            P.op("gpsimd", cast_dma(
                g_kvaB[:], g_kva[None, :].broadcast_to((128, C.KVR))),
                inc="qc0", incn=16)
            qw_const = P.c["qc0"]
            P.op("vector", lambda e: e.memset(
                kvn_bf[0][:, C.KVC * 128 + C.RD:], 0.0), inc="V")
            P.op("vector", lambda e: e.memset(
                kvn_bf[1][:, C.KVC * 128 + C.RD:], 0.0), inc="V")

            xf_free = [(), (), (), ()]
            xn_free = [(), ()]
            xnTt_free = [(), ()]
            kvps_free = [(), ()]
            trps_free = [(), ()]

            kvn_ready_v = 0
            kvn_ready_s = 0
            kvn_free = [(), ()]
            kv_done = {}
            last_ev_kr = 0

            kcn_free = ()
            krp_free = ()

            def emit_kcn_tr(t):
                nonlocal last_ev_kr, kcn_free, krp_free
                i2 = t % 2
                v_kc_t, s_kr_t = kv_done[t]
                t_tkc = None
                for c in range(C.KVC):
                    t_tkc = P.op("tensor", lambda e, c=c, i2=i2: e.matmul(
                        kcn_ps[:, c * 128:(c + 1) * 128],
                        kvn_bf[i2][:, c * 128:(c + 1) * 128], ident_bf[:],
                        start=(c == 0), stop=(c == C.KVC - 1),
                        skip_group_check=True),
                        waits=[("V", v_kc_t), ("S", s_kr_t)]
                        + (list(kcn_free) if c == 0 else []),
                        inc="T" if c == C.KVC - 1 else None)
                ev_kc = P.op("vector", lambda e, t=t: e.tensor_copy(
                    k_cnT[:].rearrange("p (c n) -> p c n", c=C.KVC)
                    [:, :, t * 128:(t + 1) * 128],
                    kcn_ps[:].rearrange("p (c n) -> p c n", c=C.KVC)),
                    waits=[("T", t_tkc)], inc="V")
                kcn_free = (("V", ev_kc),)
                t_trr = P.op("tensor", lambda e, i2=i2: e.matmul(
                    krp_ps[:, 0:128],
                    kvn_bf[i2][:, C.KVC * 128:C.KVC * 128 + 128], ident_bf[:],
                    start=True, stop=True),
                    waits=list(krp_free), inc="T")
                ev_kr = P.op("vector", lambda e, t=t: e.tensor_copy(
                    kropeT[:, t * 128:(t + 1) * 128], krp_ps[:, 0:128]),
                    waits=[("T", t_trr)], inc="V")
                krp_free = (("V", ev_kr),)
                kvn_free[i2] = (("T", t_trr),)
                last_ev_kr = ev_kr

            norm_of = {}
            qx_of = {}
            sd_of = {}
            v5_of = {}
            NG = C.D // 512
            HG = [g for g in range(NG) if (g + 1) * 512 <= C.D // 2]

            def emit_dma(t):
                i = t % 4
                qx_of[t] = P.op("gpsimd", lambda e, t=t, i=i: e.dma_start(
                    out=xf[i][:], in_=x[t * 128:(t + 1) * 128, :]),
                    waits=xf_free[i], inc=f"qg{i}", incn=16)

            def emit_bn(t, g):
                i = t % 4
                P.op("vector", lambda e, i=i, g=g: e.bn_stats(
                    bnst[:, 6 * g:6 * g + 6],
                    xf[i][:, 512 * g:512 * g + 512]),
                    waits=[(f"qg{i}", qx_of[t])], inc="V")

            def emit_agg(t):
                B = 8 * (t % 2)
                v_ag = P.op("vector", lambda e, B=B: e.bn_aggr(
                    stat[:, B + 0:B + 2], bnst[:, 0:6 * NG]),
                    waits=[("S", sd_of.get(t - 2, 0))], inc="V")
                s_sd = P.op("scalar", lambda e, B=B: e.activation(
                    stat[:, B + 2:B + 3], stat[:, B + 1:B + 2], AF.Sqrt,
                    bias=C.EPS_LN),
                    waits=[("V", v_ag), ("V", v5_of.get(t - 2, 0))], inc="S")
                sd_of[t] = s_sd
                v5 = P.op("vector", lambda e, B=B: e.reciprocal(
                    stat[:, B + 3:B + 4], stat[:, B + 2:B + 3]),
                    waits=[("S", s_sd),
                           ("S", norm_of[t - 2][1] if t >= 2 else 0)], inc="V")
                v5_of[t] = v5
                P.op("vector", lambda e, B=B: e.scalar_tensor_tensor(
                    stat[:, B + 4:B + 5], stat[:, B + 0:B + 1], -1.0,
                    stat[:, B + 3:B + 4], ALU.mult, ALU.mult), inc="V")

            def emit_norm1(t):
                i = t % 4
                i2 = t % 2
                B = 8 * (t % 2)
                vv = P.c["V"]
                s_a = P.op("scalar", lambda e, i=i, i2=i2, B=B: e.activation(
                    xn_bf[i2][:, 0:C.D // 2], xf[i][:, 0:C.D // 2], AF.Identity,
                    scale=stat[:, B + 3:B + 4], bias=stat[:, B + 4:B + 5]),
                    waits=[("V", vv)] + list(xn_free[i2]), inc="S")
                s_b = P.op("scalar", lambda e, i=i, i2=i2, B=B: e.activation(
                    xn_bf[i2][:, C.D // 2:C.D], xf[i][:, C.D // 2:C.D],
                    AF.Identity,
                    scale=stat[:, B + 3:B + 4], bias=stat[:, B + 4:B + 5]),
                    inc="S")
                norm_of[t] = (s_a, s_b)
                xf_free[i] = (("S", s_b),)

            emit_dma(0)
            for g in range(NG):
                emit_bn(0, g)
            emit_agg(0)
            emit_norm1(0)
            for t in range(C.TSB):
                i2 = t % 2
                if t + 1 < C.TSB:
                    emit_dma(t + 1)
                s_norm_a, s_norm_b = norm_of[t]
                ev_counts = []
                for g in range(C.DC // 4):
                    b = g % 2
                    sn = s_norm_a if g in HG else s_norm_b
                    t_tr = None
                    for k in range(4):
                        c = g * 4 + k
                        t_tr = P.op("tensor", lambda e, i2=i2, c=c, b=b, k=k:
                             e.matmul(trps[b][:, (c % 4) * 128:(c % 4) * 128 + 128],
                                      xn_bf[i2][:, c * 128:(c + 1) * 128],
                                      ident_bf[:], start=(k == 0), stop=(k == 3),
                                      skip_group_check=True),
                             waits=([("S", sn), ("qc0", qw_const)]
                                    + (list(trps_free[b]) if k == 0 else [])),
                             inc="T" if k == 3 else None)
                    if g % 2 == 0:
                        ev = P.op("vector", lambda e, i2=i2, g=g, b=b:
                                  e.tensor_copy(
                            xnT_tmp[i2][:, g * 512:(g + 1) * 512], trps[b][:]),
                            waits=[("T", t_tr)]
                            + list(xnTt_free[i2] if g == 0 else ()), inc="V")
                        ev = ("V", ev)
                    else:
                        ev = P.op("scalar", lambda e, i2=i2, g=g, b=b:
                                  e.activation(
                            xnT_tmp[i2][:, g * 512:(g + 1) * 512], trps[b][:],
                            AF.Identity),
                            waits=[("T", t_tr)], inc="S")
                        ev = ("S", ev)
                    trps_free[b] = (ev,)
                    ev_counts.append(ev)
                    if t + 1 < C.TSB and g < NG:
                        emit_bn(t + 1, g)
                xn_free[i2] = (ev_counts[-1],)
                if t + 1 < C.TSB:
                    emit_agg(t + 1)
                    emit_norm1(t + 1)
                kvch = _nchunks(C.KVW)
                t_kv = None
                for ci, (noff, nlen) in enumerate(kvch):
                    for c in range(C.DC):
                        need_ev = ev_counts[c // 4]
                        last = (ci == len(kvch) - 1 and c == C.DC - 1)
                        t_kv = P.op("tensor", lambda e, i2=i2, c=c, noff=noff,
                                    nlen=nlen:
                             e.matmul(kvps[i2][:, noff:noff + nlen],
                                      xnT_tmp[i2][:, c * 128:(c + 1) * 128],
                                      w_kva_bf[:, c * C.KVW + noff:
                                               c * C.KVW + noff + nlen],
                                      start=(c == 0), stop=(c == C.DC - 1)),
                             waits=([need_ev] if c % 4 == 0 else [])
                             + (list(kvps_free[i2]) if (c == 0 and noff == 0)
                                else []),
                             inc="T" if last else None)
                xnTt_free[i2] = (("T", t_kv),)
                B = 8 * (t % 2)
                s_kvf = P.op("scalar", lambda e, i2=i2: e.activation(
                    kvf[:], kvps[i2][:], AF.Identity),
                    waits=[("T", t_kv), ("V", kvn_ready_v), ("S", kvn_ready_s)],
                    inc="S")
                kvps_free[i2] = (("S", s_kvf),)
                v_ssq = P.op("vector", lambda e, B=B: e.scalar_tensor_tensor(
                    kc_tmp[:], kvf[:, 0:C.KVR], 1.0, kvf[:, 0:C.KVR],
                    ALU.mult, ALU.mult, accum_out=stat[:, B + 5:B + 6]),
                    waits=[("S", s_kvf)], inc="V")
                s_rms = P.op("scalar", lambda e, B=B: e.activation(
                    stat[:, B + 6:B + 7], stat[:, B + 5:B + 6], AF.Sqrt,
                    scale=1.0 / C.KVR),
                    waits=[("V", v_ssq)], inc="S")
                v_eps = P.op("vector", lambda e, B=B: e.tensor_scalar(
                    stat[:, B + 6:B + 7], stat[:, B + 6:B + 7], C.EPS_RMS, None,
                    ALU.add),
                    waits=[("S", s_rms)], inc="V")
                v_rin = P.op("vector", lambda e, B=B: e.reciprocal(
                    stat[:, B + 7:B + 8], stat[:, B + 6:B + 7]), inc="V")
                s_kc = P.op("scalar", lambda e, B=B: e.activation(
                    kc_tmp[:], kvf[:, 0:C.KVR], AF.Identity,
                    scale=stat[:, B + 7:B + 8]),
                    waits=[("V", v_rin)], inc="S")
                v_kc = P.op("vector", lambda e, i2=i2: e.tensor_tensor(
                    kvn_bf[i2][:, 0:C.KVR], kc_tmp[:], g_kvaB[:], ALU.mult),
                    waits=[("S", s_kc)] + list(kvn_free[i2]), inc="V")
                s_kr = P.op("scalar", lambda e, i2=i2: e.activation(
                    kvn_bf[i2][:, C.KVR:C.KVR + C.RD],
                    kvf[:, C.KVR:C.KVR + C.RD], AF.Identity), inc="S")
                kvn_ready_s = s_kr
                kvn_ready_v = v_ssq
                kv_done[t] = (v_kc, s_kr)
                if t > 0:
                    emit_kcn_tr(t - 1)
            emit_kcn_tr(C.TSB - 1)

            permd = []
            for (noff, nlen) in _nchunks(C.T_batch):
                t_p = P.op("tensor", lambda e, noff=noff, nlen=nlen: e.matmul(
                    trps[0][0:64, 0:nlen], perm_bf[:],
                    kropeT[0:64, noff:noff + nlen], start=True, stop=True),
                    waits=[("V", last_ev_kr)] + list(trps_free[0]), inc="T")
                ev = P.op("vector", lambda e, noff=noff, nlen=nlen: e.tensor_copy(
                    ropetmp[:, noff:noff + nlen], trps[0][0:64, 0:nlen]),
                    waits=[("T", t_p)], inc="V")
                trps_free[0] = (("V", ev),)
            v_a = P.op("vector", lambda e: e.tensor_tensor(
                ropeP[0:64, :], kropeT[0:64, :], cosT_bf[:], ALU.mult),
                waits=[("V", P.c["V"]), ("qc0", qw_const)], inc="V")
            v_b = P.op("vector", lambda e: e.tensor_tensor(
                ropetmp[:], ropetmp[:], sinT_bf[:], ALU.mult), inc="V")
            v_rp = P.op("vector", lambda e: e.tensor_tensor(
                ropeP[0:64, :], ropeP[0:64, :], ropetmp[:], ALU.add), inc="V")
            P.op("vector", lambda e: e.tensor_copy(
                cosQ[:], cosT_bf[:, 0:C.T_own]), inc="V")
            P.op("vector", lambda e: e.tensor_copy(
                sinQ[:], sinT_bf[:, 0:C.T_own]), inc="V")
            P.barrier()

        if phases < 2:
            ph123.close()
            _early_exit(P, nc, out, const_sb)
            return _finish(nc, top, P)
        with ExitStack() as ph2:
            E2 = ph2.enter_context
            q_cnT = E123(nc.sbuf_tensor("q_cnT", [128, C.QRC * C.T_own], BF16,
                                        side="right"))
            rstdB = E123(nc.sbuf_tensor("rstdB", [128, C.T_own], F32,
                                        side="right"))
            g_qaT = E2(nc.sbuf_tensor("g_qaT", [128, C.QRC], F32, side="right"))
            SvT = E2(nc.sbuf_tensor("SvT", [128, C.RC], F32, side="right"))
            xnT = E2(nc.sbuf_tensor("xnT", [128, C.DC * C.T_own], BF16, side="right"))
            wcol = [E2(nc.sbuf_tensor(f"wcol{i}", [128, C.DC * 128], BF16,
                                      side="right")) for i in range(2)]
            xf2 = [E2(nc.sbuf_tensor(f"p2xf{i}", [128, C.D], BF16, side="right"))
                  for i in range(3)]
            xn2 = [E2(nc.sbuf_tensor(f"p2xn{i}", [128, C.D], BF16, side="right"))
                     for i in range(1)]
            qsq = E2(nc.sbuf_tensor("qsq", [128, 512], BF16, side="right"))
            rrow = E2(nc.sbuf_tensor("rrow", [1, C.T_own], F32, side="right"))
            stat2 = E2(nc.sbuf_tensor("p2stat", [128, 8], F32, side="right"))
            bnst2 = E2(nc.sbuf_tensor("p2bnst", [128, 48], F32, side="right"))
            qct_ps = [E2(nc.psum_tensor(f"qctps{i}", [128, 512], F32))
                      for i in range(2)]
            trps2 = [E2(nc.psum_tensor(f"p2trps{i}", [128, 512], F32))
                    for i in range(2)]
            ssq_ps = E2(nc.psum_tensor("ssqps", [65, C.T_own], F32))
            rB_ps = E2(nc.psum_tensor("rBps", [128, C.T_own], F32))

            P.op("sync", lambda e: e.dma_start(
                out=g_qaT[:], in_=g_qa.rearrange("(c p) -> p c", p=128),
                allow_slow_non_contiguous=True),
                inc="qc1", incn=16)
            P.op("sync", lambda e: e.dma_start(
                out=SvT[:], in_=Sv.rearrange("(c p) -> p c", p=128),
                allow_slow_non_contiguous=True),
                inc="qc1", incn=16)
            qx_cst = P.c["qc1"]

            xf_free = [(), (), ()]
            xn_free = [(), ()]
            trps_free = [(), ()]
            xnT_ev = []
            norm_of2 = {}
            v6_of2 = {}
            prev_norm2 = [0]
            wcol0_ssq = [0]

            def emit_ln2(t):
                i = t % 3
                i2 = 0
                qx_t = P.op("gpsimd", lambda e, t=t, i=i: e.dma_start(
                    out=xf2[i][:], in_=x[t * 128:(t + 1) * 128, :]),
                    waits=xf_free[i], inc=f"qg{i}", incn=16)
                for g in range(C.D // 512):
                    P.op("vector", lambda e, i=i, g=g: e.bn_stats(
                        bnst2[:, 6 * g:6 * g + 6],
                        xf2[i][:, 512 * g:512 * g + 512]),
                        waits=[(f"qg{i}", qx_t)], inc="V")
                v_ag = P.op("vector", lambda e: e.bn_aggr(
                    stat2[:, 0:2], bnst2[:, 0:6 * (C.D // 512)]),
                    waits=[("S", prev_norm2[0])], inc="V")
                s_sd = P.op("scalar", lambda e: e.activation(
                    stat2[:, 3:4], stat2[:, 1:2], AF.Sqrt,
                    bias=C.EPS_LN), waits=[("V", v_ag)], inc="S")
                P.op("vector", lambda e: e.reciprocal(
                    stat2[:, 5:6], stat2[:, 3:4]),
                    waits=[("S", s_sd)], inc="V")
                v6 = P.op("vector", lambda e: e.scalar_tensor_tensor(
                    stat2[:, 6:7], stat2[:, 0:1], -1.0, stat2[:, 5:6],
                    ALU.mult, ALU.mult), inc="V")
                v6_of2[t] = v6
                wcol0_ssq[0] = v6

            def emit_norm2(t):
                i = t % 3
                i2 = 0
                s_norm = P.op("scalar", lambda e, i=i, i2=i2: e.activation(
                    xn2[i2][:], xf2[i][:], AF.Identity,
                    scale=stat2[:, 5:6], bias=stat2[:, 6:7]),
                    waits=[("V", v6_of2[t])] + list(xn_free[i2]), inc="S")
                xf_free[i] = (("S", s_norm),)
                norm_of2[t] = s_norm
                prev_norm2[0] = s_norm

            emit_ln2(0)
            emit_norm2(0)
            for t in range(C.TSO):
                i2 = 0
                if t + 1 < C.TSO:
                    emit_ln2(t + 1)
                s_norm = norm_of2[t]
                last_ev = 0
                for g in range(C.DC // 4):
                    b = g % 2
                    t_tr = None
                    for k in range(4):
                        c = g * 4 + k
                        t_tr = P.op("tensor", lambda e, i2=i2, c=c, b=b, k=k:
                                    e.matmul(
                            trps2[b][:, (c % 4) * 128:(c % 4) * 128 + 128],
                            xn2[i2][:, c * 128:(c + 1) * 128], ident_bf[:],
                            start=(k == 0), stop=(k == 3), skip_group_check=True),
                            waits=[("S", s_norm)]
                            + (list(trps_free[b]) if k == 0 else []),
                            inc="T" if k == 3 else None)
                    ev = P.op("vector", lambda e, t=t, g=g, b=b: e.tensor_copy(
                        xnT[:].rearrange("p (c n) -> p c n", c=C.DC)
                        [:, g * 4:(g + 1) * 4, t * 128:(t + 1) * 128],
                        trps2[b][:].rearrange("p (c n) -> p c n", c=4)),
                        waits=[("T", t_tr)], inc="V")
                    trps_free[b] = (("V", ev),)
                    last_ev = ev
                xn_free[i2] = (("V", last_ev),)
                if t + 1 < C.TSO:
                    emit_norm2(t + 1)
            xnT_done_v = P.c["V"]

            wcol_free = [(), ()]
            qct_free = [(), ()]
            ssq_last_mm = 0
            for qrc in range(C.QRC):
                wb = qrc % 2
                qw_w = P.op("gpsimd", lambda e, qrc=qrc, wb=wb: e.dma_start(
                    out=wcol[wb][:].rearrange("p (c m) -> p c m", c=C.DC),
                    in_=w_qa[:, qrc * 128:(qrc + 1) * 128].rearrange(
                        "(c p) m -> p c m", p=128)),
                    waits=wcol_free[wb], inc=f"qg{4 + wb}", incn=16)
                for qn in range(C.NQC):
                    pb = qn % 2
                    t_q = None
                    for c in range(C.DC):
                        t_q = P.op("tensor", lambda e, c=c, qn=qn, wb=wb, pb=pb:
                                   e.matmul(
                            qct_ps[pb][:],
                            wcol[wb][:, c * 128:(c + 1) * 128],
                            xnT[:, c * C.T_own + qn * 512:
                                c * C.T_own + qn * 512 + 512],
                            start=(c == 0), stop=(c == C.DC - 1)),
                            waits=([(f"qg{4 + wb}", qw_w), ("V", xnT_done_v)]
                                   + list(qct_free[pb])) if c == 0 else [],
                            inc="T" if c == C.DC - 1 else None)
                    s_sq = P.op("scalar", lambda e, pb=pb: e.activation(
                        qsq[:], qct_ps[pb][:], AF.Square),
                        waits=[("T", t_q)], inc="S")
                    v_raw = P.op("vector", lambda e, qrc=qrc, qn=qn, pb=pb:
                                 e.tensor_scalar(
                        q_cnT[:, qrc * C.T_own + qn * 512:
                              qrc * C.T_own + qn * 512 + 512],
                        qct_ps[pb][:], g_qaT[:, qrc:qrc + 1], None, ALU.mult),
                        waits=[("S", s_sq), ("qc1", qx_cst)], inc="V")
                    qct_free[pb] = (("V", v_raw),)
                    ssq_last_mm = P.op("tensor", lambda e, qrc=qrc, qn=qn:
                                       e.matmul(
                        ssq_ps[0:65, qn * 512:qn * 512 + 512],
                        ones_bf[:, 0:65], qsq[:],
                        start=(qrc == 0), stop=(qrc == C.QRC - 1),
                        skip_group_check=True),
                        waits=[("S", s_sq)], inc="T")
                wcol_free[wb] = (("T", ssq_last_mm),)
            s_r = P.op("scalar", lambda e: e.activation(
                rrow[:], ssq_ps[0:1, :], AF.Sqrt, scale=1.0 / C.QR),
                waits=[("T", ssq_last_mm)], inc="S")
            P.op("vector", lambda e: e.tensor_scalar(
                rrow[:], rrow[:], C.EPS_RMS, None, ALU.add),
                waits=[("S", s_r)], inc="V")
            v_rr = P.op("vector", lambda e: e.reciprocal(rrow[:], rrow[:]), inc="V")
            t_rb = None
            for (noff, nlen) in _nchunks(C.T_own):
                t_rb = P.op("tensor", lambda e, noff=noff, nlen=nlen: e.matmul(
                    rB_ps[:, noff:noff + nlen], ones_f32[0:1, :],
                    rrow[:, noff:noff + nlen], start=True, stop=True),
                    waits=[("V", v_rr)], inc="T")
            P.op("vector", lambda e: e.tensor_copy(rstdB[:], rB_ps[:]),
                 waits=[("T", t_rb)], inc="V")

            h1_free = [(), ()]
            for rc in range(C.RC):
                wb = rc % 2
                qw_w = P.op("gpsimd", lambda e, rc=rc, wb=wb: e.dma_start(
                    out=wcol[wb][:].rearrange("p (c m) -> p c m", c=C.DC),
                    in_=V[:, rc * 128:(rc + 1) * 128].rearrange(
                        "(c p) m -> p c m", p=128)),
                    waits=wcol_free[wb], inc=f"qg{4 + wb}", incn=16)
                for tn in range(C.NQC):
                    pb = tn % 2
                    t_h = None
                    for c in range(C.DC):
                        t_h = P.op("tensor", lambda e, c=c, tn=tn, wb=wb, pb=pb:
                                   e.matmul(
                            qct_ps[pb][:],
                            wcol[wb][:, c * 128:(c + 1) * 128],
                            xnT[:, c * C.T_own + tn * 512:
                                c * C.T_own + tn * 512 + 512],
                            start=(c == 0), stop=(c == C.DC - 1)),
                            waits=([(f"qg{4 + wb}", qw_w)] + list(h1_free[pb]))
                            if c == 0 else [],
                            inc="T" if c == C.DC - 1 else None)
                    v_h = P.op("vector", lambda e, rc=rc, tn=tn, pb=pb:
                               e.tensor_scalar(
                        h1sT[:, rc * C.T_own + tn * 512:
                             rc * C.T_own + tn * 512 + 512],
                        qct_ps[pb][:], SvT[:, rc:rc + 1], None, ALU.mult),
                        waits=[("T", t_h)], inc="V")
                    h1_free[pb] = (("V", v_h),)
                wcol_free[wb] = (("T", P.c["T"]),)
            P.barrier()

        if phases < 3:
            ph123.close()
            _early_exit(P, nc, out, const_sb)
            return _finish(nc, top, P)
        oT = E(nc.sbuf_tensor("oT", [128, C.NH * C.T_own], BF16, side="left"))
        with ExitStack() as ph3:
            E3 = ph3.enter_context
            rope2 = E3(nc.sbuf_tensor("rope2", [64, C.T_batch], BF16,
                                      side="right"))
            w_qb_bf = [E3(nc.sbuf_tensor(f"wqb{i}", [128, C.QRC * C.HW], BF16,
                                         side="right")) for i in range(2)]
            w_kvb_bf = [E3(nc.sbuf_tensor(f"wkvb{i}", [128, C.KVC * C.KW], BF16,
                                          side="right")) for i in range(2)]
            knopeT = E3(nc.sbuf_tensor("knopeT", [128, C.T_batch], BF16,
                                       side="right"))
            V1h = E3(nc.sbuf_tensor("V1h", [128, C.T_batch], BF16, side="right"))
            PT = [E3(nc.sbuf_tensor(f"PT{i}", [128, 512], BF16, side="right"))
                  for i in range(4)]
            qnopeT = E3(nc.sbuf_tensor("qnopeT", [128, C.T_own], BF16, side="right"))
            qropeM = E3(nc.sbuf_tensor("qropeM", [65, C.T_own], BF16, side="right"))
            qrtmp = E3(nc.sbuf_tensor("qrtmp", [64, C.T_own], BF16, side="right"))
            sq2 = E3(nc.sbuf_tensor("sq2", [128, max(1024, C.T_batch // 2)], BF16,
                                    side="right"))
            mrow = E3(nc.sbuf_tensor("mrow", [1, 16], F32, side="right"))
            bias_col = E3(nc.sbuf_tensor("bias_col", [128, 1], F32, side="right"))
            lrec_bf = E3(nc.sbuf_tensor("lrec_bf", [1, 512], BF16, side="right"))
            recipB = E3(nc.sbuf_tensor("recipB", [128, 512], BF16, side="right"))
            kx_ps = [E3(nc.psum_tensor(f"kxps{i}", [128, 512], F32))
                     for i in range(2)]
            s_ps = [E3(nc.psum_tensor(f"sps{i}", [128, 512], F32))
                    for i in range(4)]
            av_ps = E3(nc.psum_tensor("avps", [128, 512], F32))
            l_ps = E3(nc.psum_tensor("lps", [65, 512], F32))

            wqb_free = [(), ()]
            wkvb_free = [(), ()]
            kx_free = [(), ()]
            v_r2 = P.op("vector", lambda e: e.tensor_tensor(
                rope2[:], ropeP[0:64, :], ropeP[0:64, :], ALU.mult), inc="V")
            s_free = [(), (), (), ()]
            qsq3_free = ()
            sq2_slot_free = [(), ()]
            qsq3 = E3(nc.sbuf_tensor("qsq3", [128, 512], BF16, side="right"))
            PT_free = [(), (), (), ()]
            av_free = ()
            l_free = ()
            pending_fin = [None]

            def flush_fin(bank=0):
                if pending_fin[0] is not None:
                    pending_fin[0](bank)
                    pending_fin[0] = None
            knopeT_free = ()
            V1h_free = ()
            q_free = ()

            qw_head = [0, 0]
            for h in range(C.NH):
                wb = h % 2
                qw_head[wb] = P.op("gpsimd", lambda e, h=h, wb=wb: e.dma_start(
                    out=w_qb_bf[wb][:].rearrange("p (c m) -> p c m", c=C.QRC),
                    in_=w_qb[:, h * C.HW:(h + 1) * C.HW].rearrange(
                        "(c p) m -> p c m", p=128)),
                    waits=wqb_free[wb], inc=f"qg{wb}", incn=16)
                qw_kvb = P.op("gpsimd", lambda e, h=h, wb=wb: e.dma_start(
                    out=w_kvb_bf[wb][:].rearrange("p (c m) -> p c m", c=C.KVC),
                    in_=w_kvb[:, h * C.KW:(h + 1) * C.KW].rearrange(
                        "(c p) m -> p c m", p=128)),
                    waits=wkvb_free[wb], inc=f"qg{2 + wb}", incn=16)

                NKN = C.T_batch // 512
                kn_ev = [0] * NKN
                for n in range(NKN):
                    pb = n % 2
                    t_k = None
                    for c in range(C.KVC):
                        t_k = P.op("tensor", lambda e, c=c, n=n, wb=wb, pb=pb:
                                   e.matmul(
                            kx_ps[pb][:],
                            w_kvb_bf[wb][:, c * C.KW:c * C.KW + C.ND],
                            k_cnT[:, c * C.T_batch + n * 512:
                                  c * C.T_batch + n * 512 + 512],
                            start=(c == 0), stop=(c == C.KVC - 1)),
                            waits=([(f"qg{2 + wb}", qw_kvb)] + list(kx_free[pb])
                                   + (list(knopeT_free) if n == 0 else []))
                            if c == 0 else [],
                            inc="T" if c == C.KVC - 1 else None)
                    v_ev = P.op("vector", lambda e, n=n, pb=pb: e.tensor_copy(
                        knopeT[:, n * 512:(n + 1) * 512], kx_ps[pb][:]),
                        waits=[("T", t_k)], inc="V")
                    kx_free[pb] = (("V", v_ev),)
                    kn_ev[n] = v_ev
                    if n == 0:
                        flush_fin(bank=1)

                last_vev = 0
                for grp in range(C.TSB // 4):
                    pb = grp % 2
                    t_v = None
                    for slot in range(4):
                        ts = grp * 4 + slot
                        for c in range(C.KVC):
                            t_v = P.op("tensor", lambda e, c=c, ts=ts, slot=slot,
                                 wb=wb, pb=pb: e.matmul(
                                kx_ps[pb][:, slot * 128:(slot + 1) * 128],
                                k_cnT[:, c * C.T_batch + ts * 128:
                                      c * C.T_batch + ts * 128 + 128],
                                w_kvb_bf[wb][:, c * C.KW + C.ND:
                                             c * C.KW + C.ND + C.VD],
                                start=(slot == 0 and c == 0),
                                stop=(slot == 3 and c == C.KVC - 1),
                                skip_group_check=True),
                                waits=(list(kx_free[pb])
                                       + (list(V1h_free) if grp == 0 else []))
                                if (c == 0 and slot == 0) else [],
                                inc="T" if (slot == 3 and c == C.KVC - 1)
                                else None)
                    last_vev = P.op("vector", lambda e, grp=grp, pb=pb:
                                    e.tensor_copy(
                        V1h[:, grp * 512:(grp + 1) * 512], kx_ps[pb][:]),
                        waits=[("T", t_v)], inc="V")
                    kx_free[pb] = (("V", last_vev),)
                NKB = NKN // 2 if NKN % 2 == 0 else NKN
                wide = (NKN % 2 == 0)
                kw = 1024 if wide else 512
                s_k2_of = {}

                def emit_k2(nb, kn_ev=kn_ev):
                    sl = nb % 2
                    s_k2_of[nb] = P.op(
                        "scalar", lambda e, nb=nb, sl=sl, kw=kw: e.activation(
                            sq2[:, sl * kw:(sl + 1) * kw],
                            knopeT[:, nb * kw:(nb + 1) * kw], AF.Square),
                        waits=[("V", kn_ev[min((nb + 1) * (2 if wide else 1),
                                               NKN) - 1])]
                        + list(sq2_slot_free[sl]), inc="S")

                emit_k2(0)
                if NKB > 1:
                    emit_k2(1)
                t_qn = [None, None]
                t_qr = [None, None]
                for qn in range(C.NQC):
                    for c in range(C.QRC):
                        t_qn[qn] = P.op("tensor", lambda e, c=c, qn=qn, wb=wb:
                                        e.matmul(
                            s_ps[qn][:],
                            w_qb_bf[wb][:, c * C.HW:c * C.HW + C.ND],
                            q_cnT[:, c * C.T_own + qn * 512:
                                  c * C.T_own + qn * 512 + 512],
                            start=(c == 0), stop=(c == C.QRC - 1)),
                            waits=([(f"qg{wb}", qw_head[wb])] + list(s_free[qn])
                                   + list(q_free)) if c == 0 else [],
                            inc="T" if c == C.QRC - 1 else None)
                        t_qr[qn] = P.op("tensor", lambda e, c=c, qn=qn, wb=wb:
                                        e.matmul(
                            s_ps[2 + qn][0:64, :],
                            w_qb_bf[wb][:, c * C.HW + C.ND:c * C.HW + C.HW],
                            q_cnT[:, c * C.T_own + qn * 512:
                                  c * C.T_own + qn * 512 + 512],
                            start=(c == 0), stop=(c == C.QRC - 1)),
                            waits=list(s_free[2 + qn]) if c == 0 else [],
                            inc="T" if c == C.QRC - 1 else None)
                ev_qn = [None, None]
                ev_qr = [None, None]
                for qn in range(C.NQC):
                    ev_qn[qn] = P.op("vector", lambda e, qn=qn: e.tensor_tensor(
                        qnopeT[:, qn * 512:qn * 512 + 512], s_ps[qn][:],
                        rstdB[:, qn * 512:qn * 512 + 512], ALU.mult),
                        waits=[("T", t_qn[qn])], inc="V")
                    ev_qr[qn] = P.op("vector", lambda e, qn=qn: e.tensor_tensor(
                        qrtmp[0:64, qn * 512:qn * 512 + 512],
                        s_ps[2 + qn][0:64, :],
                        rstdB[0:64, qn * 512:qn * 512 + 512], ALU.mult),
                        waits=[("T", t_qr[qn])], inc="V")
                q_nope_v = max(x for x in [ev_qn[0], ev_qn[C.NQC - 1]])
                for qn in range(C.NQC):
                    v_s1 = P.op("scalar", lambda e, qn=qn: e.activation(
                        qsq3[:, 0:512], qnopeT[:, qn * 512:qn * 512 + 512],
                        AF.Square),
                        waits=[("V", ev_qn[qn])] + list(qsq3_free), inc="S")
                    t_n1 = P.op("tensor", lambda e, qn=qn: e.matmul(
                        s_ps[qn][0:65, :], ones_bf[:, 0:65], qsq3[:, 0:512],
                        start=True, stop=False, skip_group_check=True),
                        waits=[("S", v_s1), ("V", ev_qn[qn])]
                        + list(s_free[qn]), inc="T")
                    v_s2 = P.op("scalar", lambda e, qn=qn: e.activation(
                        qsq3[0:64, 0:512], qrtmp[0:64, qn * 512:qn * 512 + 512],
                        AF.Square),
                        waits=[("V", ev_qr[qn]), ("T", t_n1)], inc="S")
                    t_n = P.op("tensor", lambda e, qn=qn: e.matmul(
                        s_ps[qn][0:65, :], ones_bf[0:64, 0:65],
                        qsq3[0:64, 0:512],
                        start=False, stop=True, skip_group_check=True),
                        waits=[("S", v_s2)], inc="T")
                    v_m = P.op("vector", lambda e, qn=qn: e.tensor_scalar(
                        qropeM[64:65, qn * 512:qn * 512 + 512],
                        s_ps[qn][64:65, :],
                        -0.5, None, ALU.mult), waits=[("T", t_n)], inc="V")
                    s_free[qn] = (("V", v_m),)
                    qsq3_free = (("T", t_n),)
                t_pm = [None, None]
                ev_pm = [None, None]
                for qn in range(C.NQC):
                    t_pm[qn] = P.op("tensor", lambda e, qn=qn: e.matmul(
                        s_ps[2 + qn][0:64, :], perm_bf[:],
                        qrtmp[0:64, qn * 512:qn * 512 + 512],
                        start=True, stop=True),
                        waits=[("V", ev_qr[qn])], inc="T")
                    ev_pm[qn] = P.op("vector", lambda e, qn=qn: e.tensor_copy(
                        qropeM[0:64, qn * 512:qn * 512 + 512],
                        s_ps[2 + qn][0:64, :]),
                        waits=[("T", t_pm[qn])], inc="V")
                    s_free[2 + qn] = (("V", ev_pm[qn]),)
                v_c2 = P.op("vector", lambda e: e.tensor_tensor(
                    qropeM[0:64, :], qropeM[0:64, :], sinQ[:], ALU.mult),
                    inc="V")
                v_c1 = P.op("vector", lambda e: e.tensor_tensor(
                    qrtmp[:], qrtmp[:], cosQ[:], ALU.mult),
                    waits=[("S", P.c["S"])], inc="V")
                v_rot = P.op("vector", lambda e: e.tensor_tensor(
                    qropeM[0:64, :], qropeM[0:64, :], qrtmp[:], ALU.add),
                    inc="V")
                for nb in range(NKB):
                    sl = nb % 2
                    s_k2 = s_k2_of[nb]
                    t_kn = None
                    for h2 in range(2 if wide else 1):
                        b2 = (nb * (2 if wide else 1) + h2) % 2
                        P.op("tensor", lambda e, nb=nb, sl=sl, h2=h2, kw=kw,
                             b2=b2: e.matmul(
                            s_ps[b2][0:65, :], ones_bf[:, 0:65],
                            sq2[:, sl * kw + h2 * 512:sl * kw + h2 * 512 + 512],
                            start=True, stop=False, skip_group_check=True),
                            waits=[("S", s_k2)] + list(s_free[b2]))
                        t_kn = P.op("tensor", lambda e, nb=nb, h2=h2, kw=kw,
                                    b2=b2: e.matmul(
                            s_ps[b2][0:65, :], ones_bf[0:64, 0:65],
                            rope2[:, nb * kw + h2 * 512:
                                  nb * kw + h2 * 512 + 512],
                            start=False, stop=True, skip_group_check=True),
                            inc="T")
                        v_mx = P.op("vector", lambda e, nb=nb, h2=h2, b2=b2:
                                    e.reduce_max(
                            mrow[:, nb * (2 if wide else 1) + h2:
                                 nb * (2 if wide else 1) + h2 + 1],
                            s_ps[b2][0:1, :], axis=AX.X),
                            waits=[("T", t_kn)], inc="V")
                        s_free[b2] = (("V", v_mx),)
                    sq2_slot_free[sl] = (("T", t_kn),)
                    if nb + 2 < NKB:
                        emit_k2(nb + 2)
                v_mx2 = P.op("vector", lambda e: e.reduce_max(
                    mrow[:, 15:16], mrow[:, 0:NKN], axis=AX.X), inc="V")
                v_mx3 = P.op("vector", lambda e: e.tensor_scalar(
                    mrow[:, 14:15], mrow[:, 15:16],
                    -0.5 * C.SCALE, None, ALU.mult), inc="V")
                t_bc = P.op("tensor", lambda e: e.matmul(
                    s_ps[1][:, 0:1], ones_f32[0:1, :], mrow[:, 14:15],
                    start=True, stop=True, skip_group_check=True),
                    waits=[("V", v_mx3)] + list(s_free[1]), inc="T")
                v_bc = P.op("vector", lambda e: e.tensor_copy(
                    bias_col[:], s_ps[1][:, 0:1]), waits=[("T", t_bc)], inc="V")
                s_free[1] = (("V", v_bc),)
                q_ready_v = P.c["V"]

                for qc in range(C.NQC):
                    exp_done = [0] * C.KS
                    t_avl = [0] * C.KS

                    def emit_av(j, qc=qc):
                        sb = j % 4
                        P.op("tensor", lambda e, j=j, sb=sb: e.matmul(
                            l_ps[0:65, :], ones_bf[:, 0:65], PT[sb][:],
                            start=(j == 0), stop=(j == C.KS - 1),
                            skip_group_check=True),
                            waits=[("S", exp_done[j])]
                            + (list(l_free) if j == 0 else []))
                        t_avl[j] = P.op("tensor", lambda e, j=j, sb=sb, qc=qc:
                                        e.matmul(
                            av_ps[:],
                            V1h[:, j * 128:(j + 1) * 128],
                            PT[sb][:],
                            start=(j == 0), stop=(j == C.KS - 1),
                            skip_group_check=True),
                            waits=([("V", last_vev)] + list(av_free))
                            if j == 0 else [],
                            inc="T")
                        PT_free[sb] = (("T", t_avl[j]),)

                    for ks in range(C.KS):
                        sb = ks % 4
                        P.op("tensor", lambda e, ks=ks, sb=sb, qc=qc: e.matmul(
                            s_ps[sb][:],
                            knopeT[:, ks * 128:(ks + 1) * 128],
                            qnopeT[:, qc * 512:qc * 512 + 512],
                            start=True, stop=False),
                            waits=(([("V", q_nope_v)] if ks == 0 else [])
                                   + (list(s_free[sb]) if ks < 4 else [])))
                        t_s = P.op("tensor", lambda e, ks=ks, sb=sb, qc=qc:
                                   e.matmul(
                            s_ps[sb][:],
                            ropeP[:, ks * 128:(ks + 1) * 128],
                            qropeM[:, qc * 512:qc * 512 + 512],
                            start=False, stop=True),
                            waits=[("V", q_ready_v)] if ks == 0 else [],
                            inc="T")
                        exp_done[ks] = P.op("scalar", lambda e, sb=sb:
                                            e.activation(
                            PT[sb][:], s_ps[sb][:], AF.Exp,
                            scale=C.SCALE, bias=bias_col[:]),
                            waits=[("T", t_s)]
                            + ([("V", v_bc)] if ks == 0 else [])
                            + (list(PT_free[sb]) if ks < 4 else []), inc="S")
                        s_free[sb] = (("S", exp_done[ks]),)
                        if ks == 2:
                            flush_fin()
                        if ks >= 3:
                            emit_av(ks - 3)
                    emit_av(C.KS - 3)
                    emit_av(C.KS - 2)
                    emit_av(C.KS - 1)
                    t_att = P.c["T"]

                    def _fin(bank, h=h, qc=qc, t_att=t_att):
                        nonlocal av_free, l_free, kx_free
                        s_ln = P.op("scalar", lambda e: e.activation(
                            l_ps[64:65, :], l_ps[0:1, :], AF.Ln),
                            waits=[("T", t_att)], inc="S")
                        v_lr = P.op("scalar", lambda e: e.activation(
                            lrec_bf[:], l_ps[64:65, :], AF.Exp, scale=-1.0),
                            inc="S")
                        if bank == 0:
                            bps, bfree = kx_ps[1], kx_free[1]
                        else:
                            bps, bfree = s_ps[1], s_free[1]
                        t_rb = P.op("tensor", lambda e, bps=bps: e.matmul(
                            bps[:], ones_bf[0:1, :], lrec_bf[:],
                            start=True, stop=True, skip_group_check=True),
                            waits=[("S", v_lr)] + list(bfree), inc="T")
                        v_rb = P.op("vector", lambda e, bps=bps: e.tensor_copy(
                            recipB[:], bps[:]), waits=[("T", t_rb)], inc="V")
                        if bank == 0:
                            kx_free[1] = (("V", v_rb),)
                        else:
                            s_free[1] = (("V", v_rb),)
                        v_o = P.op("vector", lambda e, h=h, qc=qc:
                                   e.tensor_tensor(
                            oT[:, h * C.T_own + qc * 512:
                               h * C.T_own + qc * 512 + 512],
                            av_ps[:], recipB[:], ALU.mult), inc="V")
                        av_free = (("V", v_o),)
                        l_free = (("S", v_lr),)
                    pending_fin[0] = _fin
                wqb_free[wb] = (("T", P.c["T"]),)
                wkvb_free[wb] = (("T", P.c["T"]),)
                knopeT_free = (("T", P.c["T"]),)
                V1h_free = (("T", P.c["T"]),)
                q_free = (("T", P.c["T"]),)
            flush_fin()
            P.barrier()
        ph123.close()

        if phases < 4:
            _early_exit(P, nc, out, const_sb)
            return _finish(nc, top, P)
        with ExitStack() as ph4:
            E4 = ph4.enter_context
            wo_buf = [E4(nc.sbuf_tensor(f"wo{i}", [128, C.NH * 512], BF16,
                                        side="right")) for i in range(2)]
            u_tmp = E4(nc.sbuf_tensor("u_tmp", [128, 4 * C.RANK], BF16,
                                      side="right"))
            uT = [E4(nc.sbuf_tensor(f"uT{i}", [128, C.RC * 512], BF16,
                                    side="right")) for i in range(2)]
            xres = [E4(nc.sbuf_tensor(f"xres{i}", [128, 512], F32, side="right"))
                    for i in range(2)]
            stage = [E4(nc.sbuf_tensor(f"stage{i}", [128, 512], F32, side="right"))
                     for i in range(2)]
            out_ps = [E4(nc.psum_tensor(f"outps{i}", [128, 512], F32))
                      for i in range(2)]
            trps4 = [E4(nc.psum_tensor(f"p4trps{i}", [128, 512], F32))
                    for i in range(2)]

            wo_free = [(), ()]
            uT_free = [(), ()]
            utmp_free = ()
            xres_free = [(), ()]
            stage_free = [(), ()]
            outps_free = [(), ()]
            trps_free = [(), ()]
            for j in range(C.NJ):
                wb = j % 2
                qw_wo = P.op("gpsimd", lambda e, j=j, wb=wb: e.dma_start(
                    out=wo_buf[wb][:].rearrange("p (c m) -> p c m", c=C.NH),
                    in_=w_o[:, j * 512:(j + 1) * 512].rearrange(
                        "(c p) m -> p c m", p=128)),
                    waits=wo_free[wb], inc=f"qg{wb}", incn=16)
                qw_u = P.op("gpsimd", lambda e, j=j: e.dma_start(
                    out=u_tmp[:].rearrange("p (c m) -> p c m", c=4),
                    in_=U[j * 512:(j + 1) * 512, :].rearrange(
                        "(c p) m -> p c m", p=128)),
                    waits=utmp_free, inc="qc0", incn=16)
                n_tr = 0
                uev = []
                for rb in range(C.RC):
                    pb = rb % 2
                    t_tr = None
                    for c4 in range(4):
                        t_tr = P.op("tensor", lambda e, rb=rb, c4=c4, pb=pb:
                                    e.matmul(
                            trps4[pb][:, c4 * 128:(c4 + 1) * 128],
                            u_tmp[:, c4 * C.RANK + rb * 128:
                                  c4 * C.RANK + rb * 128 + 128],
                            ident_bf[:], start=(c4 == 0), stop=(c4 == 3),
                            skip_group_check=True),
                            waits=([("qc0", qw_u)] + list(trps_free[pb])
                                   + (list(uT_free[wb]) if rb == 0 else []))
                            if c4 == 0 else [],
                            inc="T" if c4 == 3 else None)
                    ev = P.op("vector", lambda e, rb=rb, pb=pb, wb=wb:
                              e.tensor_copy(
                        uT[wb][:, rb * 512:(rb + 1) * 512], trps4[pb][:]),
                        waits=[("T", t_tr)], inc="V")
                    trps_free[pb] = (("V", ev),)
                    uev.append(ev)
                uT_ready = uev[-1]
                utmp_free = (("T", P.c["T"]),)
                for tc in range(C.TSO):
                    pb = tc % 2
                    qx_x = P.op("sync", lambda e, j=j, tc=tc, pb=pb: e.dma_start(
                        out=xres[pb][:],
                        in_=x[tc * 128:(tc + 1) * 128, j * 512:(j + 1) * 512]),
                        waits=xres_free[pb], inc=f"qs{2 + pb}", incn=16)
                    for c in range(C.NH):
                        P.op("tensor", lambda e, c=c, j=j, tc=tc, wb=wb, pb=pb:
                             e.matmul(
                            out_ps[pb][:],
                            oT[:, c * C.T_own + tc * 128:
                               c * C.T_own + tc * 128 + 128],
                            wo_buf[wb][:, c * 512:(c + 1) * 512],
                            start=(c == 0), stop=False),
                            waits=([(f"qg{wb}", qw_wo)] + list(outps_free[pb]))
                            if c == 0 else [])
                    t_o = None
                    for rc in range(C.RC):
                        t_o = P.op("tensor", lambda e, rc=rc, tc=tc, wb=wb, pb=pb:
                             e.matmul(
                            out_ps[pb][:],
                            h1sT[:, rc * C.T_own + tc * 128:
                                 rc * C.T_own + tc * 128 + 128],
                            uT[wb][:, rc * 512:(rc + 1) * 512],
                            start=False, stop=(rc == C.RC - 1)),
                            waits=[("V", uT_ready)] if rc == 0 else [],
                            inc="T" if rc == C.RC - 1 else None)
                    v_st = P.op("vector", lambda e, pb=pb: e.tensor_tensor(
                        stage[pb][:], out_ps[pb][:], xres[pb][:], ALU.add),
                        waits=[("T", t_o), (f"qs{2 + pb}", qx_x)] + list(stage_free[pb]),
                        inc="V")
                    outps_free[pb] = (("V", v_st),)
                    xres_free[pb] = (("V", v_st),)
                    qo_st = P.op("sync", lambda e, j=j, tc=tc, pb=pb: e.dma_start(
                        out=out[tc * 128:(tc + 1) * 128, j * 512:(j + 1) * 512],
                        in_=stage[pb][:]),
                        waits=[("V", v_st)], inc=f"qo{pb}", incn=16)
                    stage_free[pb] = ((f"qo{pb}", qo_st),)
                uT_free[wb] = (("T", P.c["T"]),)
                wo_free[wb] = (("T", P.c["T"]),)

            P.wait("sync", ("qo0", P.c["qo0"]), ("qo1", P.c["qo1"]))

        return _finish(nc, top, P)


def _early_exit(P, nc, out, const_sb):
    P.barrier()
    qo = P.op("sync", lambda e: e.dma_start(
        out=out[0:128, 0:8], in_=const_sb[:, 0:8]), inc="qo0", incn=16)
    P.wait("sync", ("qo0", qo))


def _finish(nc, top, P):
    with nc.Block() as block:
        block.sync(lambda e: [f(e) for f in P.sched["sync"]])
        block.gpsimd(lambda e: [f(e) for f in P.sched["gpsimd"]])
        block.vector(lambda e: [f(e) for f in P.sched["vector"]])
        block.scalar(lambda e: [f(e) for f in P.sched["scalar"]])
        block.tensor(lambda e: [f(e) for f in P.sched["tensor"]])
    top.close()
    return nc


def make_tables(cfg: Cfg):
    t = np.arange(cfg.T_batch, dtype=np.float64)
    inv = 1.0 / (10000.0 ** (np.arange(0, cfg.RD, 2, dtype=np.float64) / cfg.RD))
    freqs = np.outer(t, inv)
    emb = np.concatenate([freqs, freqs], axis=-1)
    cos = np.cos(emb).T.astype(np.float32)
    sin = np.sin(emb).T.astype(np.float32)
    return np.ascontiguousarray(cos), np.ascontiguousarray(sin)


def make_consts():
    ident = np.eye(128, dtype=np.float32)
    onesm = np.ones((128, 128), dtype=np.float32)
    perm = np.zeros((64, 64), dtype=np.float32)
    for i in range(32):
        perm[i + 32, i] = -1.0
        perm[i, i + 32] = 1.0
    return ident, onesm, perm


def make_in_maps(cfg, weights, x_full, n_rot):
    cos, sin = make_tables(cfg)
    ident, onesm, perm = make_consts()
    maps = []
    for r in n_rot:
        xr = np.ascontiguousarray(np.roll(x_full, -r, axis=0))
        cr = np.ascontiguousarray(np.roll(cos, -r, axis=1))
        sr = np.ascontiguousarray(np.roll(sin, -r, axis=1))
        m = dict(weights)
        m.update(x=xr, cosT=cr, sinT=sr, ident=ident, ones=onesm, perm=perm)
        maps.append(m)
    return maps


def kernel(L0_attn_hidden_states, L0_attn_w_qa, L0_attn_g_qa, L0_attn_w_qb,
           L0_attn_w_kva, L0_attn_g_kva, L0_attn_w_kvb, L0_attn_w_o,
           L0_ffn_U, L0_ffn_S, L0_ffn_V):
    cfg = FULL
    nc = build_program(cfg)
    weights = {
        "w_qa": np.ascontiguousarray(L0_attn_w_qa, dtype=np.float32),
        "g_qa": np.ascontiguousarray(L0_attn_g_qa, dtype=np.float32),
        "w_qb": np.ascontiguousarray(L0_attn_w_qb, dtype=np.float32),
        "w_kva": np.ascontiguousarray(L0_attn_w_kva, dtype=np.float32),
        "g_kva": np.ascontiguousarray(L0_attn_g_kva, dtype=np.float32),
        "w_kvb": np.ascontiguousarray(L0_attn_w_kvb, dtype=np.float32),
        "w_o": np.ascontiguousarray(L0_attn_w_o, dtype=np.float32),
        "U": np.ascontiguousarray(L0_ffn_U, dtype=np.float32),
        "Sv": np.ascontiguousarray(L0_ffn_S, dtype=np.float32),
        "V": np.ascontiguousarray(L0_ffn_V, dtype=np.float32),
    }
    x = np.asarray(L0_attn_hidden_states, dtype=np.float32)
    in_maps = []
    for core in range(8):
        b, jq = core // 4, core % 4
        in_maps.extend(make_in_maps(cfg, weights, x[b], [jq * cfg.T_own]))
    res = run_bass_kernel_spmd(nc, in_maps, core_ids=list(range(8)))
    outp = np.empty((2, 4096, 4096), dtype=np.float32)
    for core in range(8):
        b, jq = core // 4, core % 4
        outp[b, jq * cfg.T_own:(jq + 1) * cfg.T_own, :] = res.results[core]["out"]
    return outp


# version 31 (final)
# speedup vs baseline: 1.2245x; 1.0025x over previous
import math
import numpy as np
from contextlib import ExitStack

import concourse.bass as bass
import concourse.mybir as mybir
from concourse.bass_utils import run_bass_kernel_spmd

F32 = mybir.dt.float32
BF16 = mybir.dt.bfloat16
AF = mybir.ActivationFunctionType
ALU = mybir.AluOpType
AX = mybir.AxisListType


class Cfg:
    def __init__(self, D=4096, QR=1536, KVR=512, NH=32, RANK=1024,
                 T_own=1024, T_batch=4096):
        self.D, self.QR, self.KVR, self.NH, self.RANK = D, QR, KVR, NH, RANK
        self.T_own, self.T_batch = T_own, T_batch
        self.RD, self.ND, self.VD = 64, 128, 128
        self.QK = self.ND + self.RD
        self.DC = D // 128
        self.QRC = QR // 128
        self.KVC = KVR // 128
        self.RC = RANK // 128
        self.TSB = T_batch // 128
        self.TSO = T_own // 128
        self.NQC = T_own // 512
        self.KS = T_batch // 128
        self.NJ = D // 512
        self.KVW = KVR + self.RD
        self.HW = self.ND + self.RD
        self.KW = self.ND + self.VD
        self.EPS_LN = 1e-5
        self.EPS_RMS = 1e-6
        self.SCALE = self.QK ** (-0.5)
        assert T_own % 512 == 0 and T_batch % 256 == 0


FULL = Cfg()


def _nchunks(total, step=512):
    out = []
    off = 0
    while off < total:
        out.append((off, min(step, total - off)))
        off += step
    return out


class Plan:

    ENGINES = ("sync", "gpsimd", "vector", "scalar", "tensor")

    def __init__(self, nc, ctx, sem_names):
        self.nc = nc
        self.sems = {}
        self.c = {}
        for s in sem_names:
            self.sems[s] = ctx.enter_context(nc.semaphore(s))
            self.c[s] = 0
        self.sched = {e: [] for e in self.ENGINES}

    OWN = {"vector": "V", "scalar": "S"}

    def op(self, eng, build, waits=(), inc=None, incn=1):
        own = self.OWN.get(eng)
        if own is not None:
            waits = tuple(waits) + ((own, self.c[own]),)
            if inc is None:
                inc = own
        waits = tuple((s, v) for (s, v) in waits if v > 0)
        sems = self.sems

        def run(e, waits=waits, build=build, inc=inc, incn=incn):
            for s, v in waits:
                e.wait_ge(sems[s], v)
            ins = build(e)
            if inc is not None:
                ins.then_inc(sems[inc], incn)
        self.sched[eng].append(run)
        if inc is not None:
            self.c[inc] += incn
            return self.c[inc]
        return None

    def wait(self, eng, *waits):
        waits = tuple((s, v) for (s, v) in waits if v > 0)
        if not waits:
            return
        sems = self.sems

        def run(e, waits=waits):
            for s, v in waits:
                e.wait_ge(sems[s], v)
        self.sched[eng].append(run)

    def barrier(self):
        snap = tuple((s, v) for s, v in self.c.items() if v > 0)
        for e in self.ENGINES:
            self.wait(e, *snap)


def build_program(cfg: Cfg, phases=4):
    nc = bass.Bass("TRN2", target_bir_lowering=False, debug=False)
    C = cfg
    DT = {}

    def din(name, shape):
        DT[name] = nc.dram_tensor(name, list(shape), F32, kind="ExternalInput").ap()
        return DT[name]

    x = din("x", (C.T_batch, C.D))
    w_qa = din("w_qa", (C.D, C.QR))
    g_qa = din("g_qa", (C.QR,))
    w_qb = din("w_qb", (C.QR, C.NH * C.HW))
    w_kva = din("w_kva", (C.D, C.KVW))
    g_kva = din("g_kva", (C.KVR,))
    w_kvb = din("w_kvb", (C.KVR, C.NH * C.KW))
    w_o = din("w_o", (C.NH * C.VD, C.D))
    U = din("U", (C.D, C.RANK))
    Sv = din("Sv", (C.RANK,))
    V = din("V", (C.D, C.RANK))
    cosT = din("cosT", (C.RD, C.T_batch))
    sinT = din("sinT", (C.RD, C.T_batch))
    ident = din("ident", (128, 128))
    ones = din("ones", (128, 128))
    perm = din("perm", (64, 64))
    out = nc.dram_tensor("out", [C.T_own, C.D], F32, kind="ExternalOutput").ap()

    with ExitStack() as top:
        E = top.enter_context
        P = Plan(nc, top, ["qs0", "qs1", "qs2", "qs3", "qg0", "qg1", "qg2", "qg3",
                           "qg4", "qg5", "qc0", "qc1", "qo0", "qo1",
                           "T", "V", "S"])

        ident_bf = E(nc.sbuf_tensor("ident_bf", [128, 128], BF16, side="left"))
        ones_bf = E(nc.sbuf_tensor("ones_bf", [128, 128], BF16, side="left"))
        ones_f32 = E(nc.sbuf_tensor("ones_f32", [128, 128], F32, side="left"))
        perm_bf = E(nc.sbuf_tensor("perm_bf", [64, 64], BF16, side="left"))
        h1sT = E(nc.sbuf_tensor("h1sT", [128, C.RC * C.T_own], BF16, side="left"))
        ph123 = ExitStack()
        E123 = ph123.enter_context
        cosQ = E123(nc.sbuf_tensor("cosQ", [64, C.T_own], BF16, side="right"))
        sinQ = E123(nc.sbuf_tensor("sinQ", [64, C.T_own], BF16, side="right"))
        ropeP = E123(nc.sbuf_tensor("ropeP", [65, C.T_batch], BF16, side="right"))
        k_cnT = E123(nc.sbuf_tensor("k_cnT", [128, C.KVC * C.T_batch], BF16,
                                    side="right"))

        def cast_dma(dst_ap, src_ap):
            return lambda e: e.dma_start(out=dst_ap, in_=src_ap)

        P.op("gpsimd", cast_dma(ident_bf[:], ident[:, :]), inc="qc0", incn=16)
        P.op("gpsimd", cast_dma(ones_bf[:], ones[:, :]), inc="qc0", incn=16)
        P.op("gpsimd", cast_dma(perm_bf[:], perm[:, :]), inc="qc0", incn=16)
        P.op("gpsimd", lambda e: e.dma_start(out=ones_f32[:], in_=ones[:, :]),
             inc="qc0", incn=16)
        P.op("vector", lambda e: e.memset(ropeP[64:65, :], 1.0), inc="V")
        const_vals = [C.EPS_LN, 1.0 / C.KVR, 1.0 / C.QR, C.SCALE, 1.0, 0.0,
                      1.0 / C.D, C.EPS_RMS, -1.0, -0.5]
        const_sb = E(nc.sbuf_tensor("const_sb", [128, len(const_vals)], F32,
                                    side="left"))
        negsc = E(nc.sbuf_tensor("negsc", [1, 128], F32, side="left"))
        cv = 0
        for i, val in enumerate(const_vals):
            cv = P.op("vector", lambda e, i=i, val=val: e.memset(
                const_sb[:, i:i + 1], val), inc="V")
            nc.const_aps.register(F32, val, const_sb[:, i:i + 1])
        cv = P.op("vector", lambda e: e.memset(
            negsc[:], -0.5 * C.SCALE), inc="V")
        P.wait("scalar", ("V", cv))
        P.wait("vector", ("V", cv))
        P.wait("tensor", ("V", cv))

        with ExitStack() as ph1:
            E1 = ph1.enter_context
            cosT_bf = E1(nc.sbuf_tensor("cosT_bf", [64, C.T_batch], BF16,
                                        side="right"))
            sinT_bf = E1(nc.sbuf_tensor("sinT_bf", [64, C.T_batch], BF16,
                                        side="right"))
            w_kva_bf = E1(nc.sbuf_tensor("w_kva_bf", [128, C.DC * C.KVW], BF16,
                                         side="right"))
            g_kvaB = E1(nc.sbuf_tensor("g_kvaB", [128, C.KVR], F32, side="right"))
            xf = [E1(nc.sbuf_tensor(f"xf{i}", [128, C.D], BF16, side="right"))
                  for i in range(4)]
            xn_bf = [E1(nc.sbuf_tensor(f"xn{i}", [128, C.D], BF16, side="right"))
                     for i in range(2)]
            xnT_tmp = [E1(nc.sbuf_tensor(f"xnTt{i}", [128, C.DC * 128], BF16,
                                         side="right")) for i in range(2)]
            kvf = E1(nc.sbuf_tensor("kvf", [128, C.KVW], F32, side="right"))
            kvn_bf = [E1(nc.sbuf_tensor(f"kvn_bf{j}", [128, C.KVC * 128 + 128],
                                        BF16, side="right")) for j in range(2)]
            kc_tmp = E1(nc.sbuf_tensor("kc_tmp", [128, C.KVR], BF16, side="right"))
            kropeT = E1(nc.sbuf_tensor("kropeT", [128, C.T_batch], BF16, side="right"))
            ropetmp = E1(nc.sbuf_tensor("ropetmp", [64, C.T_batch], BF16, side="right"))
            stat = E1(nc.sbuf_tensor("stat", [128, 16], F32, side="right"))
            bnst = E1(nc.sbuf_tensor("bnst", [128, 48], F32, side="right"))
            kvps = [E1(nc.psum_tensor(f"kvps{i}", [128, C.KVW], F32)) for i in range(2)]
            trps = [E1(nc.psum_tensor(f"trps{i}", [128, 512], F32)) for i in range(2)]
            kcn_ps = E1(nc.psum_tensor("kcnps", [128, C.KVC * 128], F32))
            krp_ps = E1(nc.psum_tensor("krpps", [128, 128], F32))

            P.op("gpsimd", cast_dma(cosT_bf[:], cosT[:, :]), inc="qc0", incn=16)
            P.op("gpsimd", cast_dma(sinT_bf[:], sinT[:, :]), inc="qc0", incn=16)
            P.op("gpsimd", cast_dma(
                w_kva_bf[:].rearrange("p (c m) -> p c m", c=C.DC),
                w_kva.rearrange("(c p) m -> p c m", p=128)), inc="qc0", incn=16)
            P.op("gpsimd", cast_dma(
                g_kvaB[:], g_kva[None, :].broadcast_to((128, C.KVR))),
                inc="qc0", incn=16)
            qw_const = P.c["qc0"]
            P.op("vector", lambda e: e.memset(
                kvn_bf[0][:, C.KVC * 128 + C.RD:], 0.0), inc="V")
            P.op("vector", lambda e: e.memset(
                kvn_bf[1][:, C.KVC * 128 + C.RD:], 0.0), inc="V")

            xf_free = [(), (), (), ()]
            xn_free = [(), ()]
            xnTt_free = [(), ()]
            kvps_free = [(), ()]
            trps_free = [(), ()]

            kvn_ready_v = 0
            kvn_ready_s = 0
            kvn_free = [(), ()]
            kv_done = {}
            last_ev_kr = 0

            kcn_free = ()
            krp_free = ()

            def emit_kcn_tr(t):
                nonlocal last_ev_kr, kcn_free, krp_free
                i2 = t % 2
                v_kc_t, s_kr_t = kv_done[t]
                t_tkc = None
                for c in range(C.KVC):
                    t_tkc = P.op("tensor", lambda e, c=c, i2=i2: e.matmul(
                        kcn_ps[:, c * 128:(c + 1) * 128],
                        kvn_bf[i2][:, c * 128:(c + 1) * 128], ident_bf[:],
                        start=(c == 0), stop=(c == C.KVC - 1),
                        skip_group_check=True),
                        waits=[("V", v_kc_t), ("S", s_kr_t)]
                        + (list(kcn_free) if c == 0 else []),
                        inc="T" if c == C.KVC - 1 else None)
                ev_kc = P.op("vector", lambda e, t=t: e.tensor_copy(
                    k_cnT[:].rearrange("p (c n) -> p c n", c=C.KVC)
                    [:, :, t * 128:(t + 1) * 128],
                    kcn_ps[:].rearrange("p (c n) -> p c n", c=C.KVC)),
                    waits=[("T", t_tkc)], inc="V")
                kcn_free = (("V", ev_kc),)
                t_trr = P.op("tensor", lambda e, i2=i2: e.matmul(
                    krp_ps[:, 0:128],
                    kvn_bf[i2][:, C.KVC * 128:C.KVC * 128 + 128], ident_bf[:],
                    start=True, stop=True),
                    waits=list(krp_free), inc="T")
                ev_kr = P.op("vector", lambda e, t=t: e.tensor_copy(
                    kropeT[:, t * 128:(t + 1) * 128], krp_ps[:, 0:128]),
                    waits=[("T", t_trr)], inc="V")
                krp_free = (("V", ev_kr),)
                kvn_free[i2] = (("T", t_trr),)
                last_ev_kr = ev_kr

            norm_of = {}
            qx_of = {}
            sd_of = {}
            v5_of = {}
            NG = C.D // 512
            HG = [g for g in range(NG) if (g + 1) * 512 <= C.D // 2]

            def emit_dma(t):
                i = t % 4
                qx_of[t] = P.op("gpsimd", lambda e, t=t, i=i: e.dma_start(
                    out=xf[i][:], in_=x[t * 128:(t + 1) * 128, :]),
                    waits=xf_free[i], inc=f"qg{i}", incn=16)

            def emit_bn(t, g):
                i = t % 4
                P.op("vector", lambda e, i=i, g=g: e.bn_stats(
                    bnst[:, 6 * g:6 * g + 6],
                    xf[i][:, 512 * g:512 * g + 512]),
                    waits=[(f"qg{i}", qx_of[t])], inc="V")

            def emit_agg(t):
                B = 8 * (t % 2)
                v_ag = P.op("vector", lambda e, B=B: e.bn_aggr(
                    stat[:, B + 0:B + 2], bnst[:, 0:6 * NG]),
                    waits=[("S", sd_of.get(t - 2, 0))], inc="V")
                s_sd = P.op("scalar", lambda e, B=B: e.activation(
                    stat[:, B + 2:B + 3], stat[:, B + 1:B + 2], AF.Sqrt,
                    bias=C.EPS_LN),
                    waits=[("V", v_ag), ("V", v5_of.get(t - 2, 0))], inc="S")
                sd_of[t] = s_sd
                v5 = P.op("vector", lambda e, B=B: e.reciprocal(
                    stat[:, B + 3:B + 4], stat[:, B + 2:B + 3]),
                    waits=[("S", s_sd),
                           ("S", norm_of[t - 2][1] if t >= 2 else 0)], inc="V")
                v5_of[t] = v5
                P.op("vector", lambda e, B=B: e.scalar_tensor_tensor(
                    stat[:, B + 4:B + 5], stat[:, B + 0:B + 1], -1.0,
                    stat[:, B + 3:B + 4], ALU.mult, ALU.mult), inc="V")

            def emit_norm1(t):
                i = t % 4
                i2 = t % 2
                B = 8 * (t % 2)
                vv = P.c["V"]
                s_a = P.op("scalar", lambda e, i=i, i2=i2, B=B: e.activation(
                    xn_bf[i2][:, 0:C.D // 2], xf[i][:, 0:C.D // 2], AF.Identity,
                    scale=stat[:, B + 3:B + 4], bias=stat[:, B + 4:B + 5]),
                    waits=[("V", vv)] + list(xn_free[i2]), inc="S")
                s_b = P.op("scalar", lambda e, i=i, i2=i2, B=B: e.activation(
                    xn_bf[i2][:, C.D // 2:C.D], xf[i][:, C.D // 2:C.D],
                    AF.Identity,
                    scale=stat[:, B + 3:B + 4], bias=stat[:, B + 4:B + 5]),
                    inc="S")
                norm_of[t] = (s_a, s_b)
                xf_free[i] = (("S", s_b),)

            emit_dma(0)
            for g in range(NG):
                emit_bn(0, g)
            emit_agg(0)
            emit_norm1(0)
            for t in range(C.TSB):
                i2 = t % 2
                if t + 1 < C.TSB:
                    emit_dma(t + 1)
                s_norm_a, s_norm_b = norm_of[t]
                ev_counts = []
                for g in range(C.DC // 4):
                    b = g % 2
                    sn = s_norm_a if g in HG else s_norm_b
                    t_tr = None
                    for k in range(4):
                        c = g * 4 + k
                        t_tr = P.op("tensor", lambda e, i2=i2, c=c, b=b, k=k:
                             e.matmul(trps[b][:, (c % 4) * 128:(c % 4) * 128 + 128],
                                      xn_bf[i2][:, c * 128:(c + 1) * 128],
                                      ident_bf[:], start=(k == 0), stop=(k == 3),
                                      skip_group_check=True),
                             waits=([("S", sn), ("qc0", qw_const)]
                                    + (list(trps_free[b]) if k == 0 else [])),
                             inc="T" if k == 3 else None)
                    if g % 2 == 0:
                        ev = P.op("vector", lambda e, i2=i2, g=g, b=b:
                                  e.tensor_copy(
                            xnT_tmp[i2][:, g * 512:(g + 1) * 512], trps[b][:]),
                            waits=[("T", t_tr)]
                            + list(xnTt_free[i2] if g == 0 else ()), inc="V")
                        ev = ("V", ev)
                    else:
                        ev = P.op("scalar", lambda e, i2=i2, g=g, b=b:
                                  e.activation(
                            xnT_tmp[i2][:, g * 512:(g + 1) * 512], trps[b][:],
                            AF.Identity),
                            waits=[("T", t_tr)], inc="S")
                        ev = ("S", ev)
                    trps_free[b] = (ev,)
                    ev_counts.append(ev)
                    if t + 1 < C.TSB and g < NG:
                        emit_bn(t + 1, g)
                xn_free[i2] = (ev_counts[-1],)
                if t + 1 < C.TSB:
                    emit_agg(t + 1)
                    emit_norm1(t + 1)
                kvch = _nchunks(C.KVW)
                t_kv = None
                for ci, (noff, nlen) in enumerate(kvch):
                    for c in range(C.DC):
                        need_ev = ev_counts[c // 4]
                        last = (ci == len(kvch) - 1 and c == C.DC - 1)
                        t_kv = P.op("tensor", lambda e, i2=i2, c=c, noff=noff,
                                    nlen=nlen:
                             e.matmul(kvps[i2][:, noff:noff + nlen],
                                      xnT_tmp[i2][:, c * 128:(c + 1) * 128],
                                      w_kva_bf[:, c * C.KVW + noff:
                                               c * C.KVW + noff + nlen],
                                      start=(c == 0), stop=(c == C.DC - 1)),
                             waits=([need_ev] if c % 4 == 0 else [])
                             + (list(kvps_free[i2]) if (c == 0 and noff == 0)
                                else []),
                             inc="T" if last else None)
                xnTt_free[i2] = (("T", t_kv),)
                B = 8 * (t % 2)
                s_kvf = P.op("scalar", lambda e, i2=i2: e.activation(
                    kvf[:], kvps[i2][:], AF.Identity),
                    waits=[("T", t_kv), ("V", kvn_ready_v), ("S", kvn_ready_s)],
                    inc="S")
                kvps_free[i2] = (("S", s_kvf),)
                v_ssq = P.op("vector", lambda e, B=B: e.scalar_tensor_tensor(
                    kc_tmp[:], kvf[:, 0:C.KVR], 1.0, kvf[:, 0:C.KVR],
                    ALU.mult, ALU.mult, accum_out=stat[:, B + 5:B + 6]),
                    waits=[("S", s_kvf)], inc="V")
                s_rms = P.op("scalar", lambda e, B=B: e.activation(
                    stat[:, B + 6:B + 7], stat[:, B + 5:B + 6], AF.Sqrt,
                    scale=1.0 / C.KVR),
                    waits=[("V", v_ssq)], inc="S")
                v_eps = P.op("vector", lambda e, B=B: e.tensor_scalar(
                    stat[:, B + 6:B + 7], stat[:, B + 6:B + 7], C.EPS_RMS, None,
                    ALU.add),
                    waits=[("S", s_rms)], inc="V")
                v_rin = P.op("vector", lambda e, B=B: e.reciprocal(
                    stat[:, B + 7:B + 8], stat[:, B + 6:B + 7]), inc="V")
                s_kc = P.op("scalar", lambda e, B=B: e.activation(
                    kc_tmp[:], kvf[:, 0:C.KVR], AF.Identity,
                    scale=stat[:, B + 7:B + 8]),
                    waits=[("V", v_rin)], inc="S")
                v_kc = P.op("vector", lambda e, i2=i2: e.tensor_tensor(
                    kvn_bf[i2][:, 0:C.KVR], kc_tmp[:], g_kvaB[:], ALU.mult),
                    waits=[("S", s_kc)] + list(kvn_free[i2]), inc="V")
                s_kr = P.op("scalar", lambda e, i2=i2: e.activation(
                    kvn_bf[i2][:, C.KVR:C.KVR + C.RD],
                    kvf[:, C.KVR:C.KVR + C.RD], AF.Identity), inc="S")
                kvn_ready_s = s_kr
                kvn_ready_v = v_ssq
                kv_done[t] = (v_kc, s_kr)
                if t > 0:
                    emit_kcn_tr(t - 1)
            emit_kcn_tr(C.TSB - 1)

            permd = []
            for (noff, nlen) in _nchunks(C.T_batch):
                t_p = P.op("tensor", lambda e, noff=noff, nlen=nlen: e.matmul(
                    trps[0][0:64, 0:nlen], perm_bf[:],
                    kropeT[0:64, noff:noff + nlen], start=True, stop=True),
                    waits=[("V", last_ev_kr)] + list(trps_free[0]), inc="T")
                ev = P.op("vector", lambda e, noff=noff, nlen=nlen: e.tensor_copy(
                    ropetmp[:, noff:noff + nlen], trps[0][0:64, 0:nlen]),
                    waits=[("T", t_p)], inc="V")
                trps_free[0] = (("V", ev),)
            v_a = P.op("vector", lambda e: e.tensor_tensor(
                ropeP[0:64, :], kropeT[0:64, :], cosT_bf[:], ALU.mult),
                waits=[("V", P.c["V"]), ("qc0", qw_const)], inc="V")
            v_b = P.op("vector", lambda e: e.tensor_tensor(
                ropetmp[:], ropetmp[:], sinT_bf[:], ALU.mult), inc="V")
            v_rp = P.op("vector", lambda e: e.tensor_tensor(
                ropeP[0:64, :], ropeP[0:64, :], ropetmp[:], ALU.add), inc="V")
            P.op("vector", lambda e: e.tensor_copy(
                cosQ[:], cosT_bf[:, 0:C.T_own]), inc="V")
            P.op("vector", lambda e: e.tensor_copy(
                sinQ[:], sinT_bf[:, 0:C.T_own]), inc="V")
            P.barrier()

        if phases < 2:
            ph123.close()
            _early_exit(P, nc, out, const_sb)
            return _finish(nc, top, P)
        with ExitStack() as ph2:
            E2 = ph2.enter_context
            q_cnT = E123(nc.sbuf_tensor("q_cnT", [128, C.QRC * C.T_own], BF16,
                                        side="right"))
            rstdB = E123(nc.sbuf_tensor("rstdB", [128, C.T_own], F32,
                                        side="right"))
            g_qaT = E2(nc.sbuf_tensor("g_qaT", [128, C.QRC], F32, side="right"))
            SvT = E2(nc.sbuf_tensor("SvT", [128, C.RC], F32, side="right"))
            xnT = E2(nc.sbuf_tensor("xnT", [128, C.DC * C.T_own], BF16, side="right"))
            wcol = [E2(nc.sbuf_tensor(f"wcol{i}", [128, C.DC * 128], BF16,
                                      side="right")) for i in range(3)]
            xf2 = [E2(nc.sbuf_tensor(f"p2xf{i}", [128, C.D], BF16, side="right"))
                  for i in range(2)]
            xn2 = [E2(nc.sbuf_tensor(f"p2xn{i}", [128, C.D], BF16, side="right"))
                     for i in range(1)]
            qsq = E2(nc.sbuf_tensor("qsq", [128, 512], BF16, side="right"))
            rrow = E2(nc.sbuf_tensor("rrow", [1, C.T_own], F32, side="right"))
            stat2 = E2(nc.sbuf_tensor("p2stat", [128, 8], F32, side="right"))
            bnst2 = E2(nc.sbuf_tensor("p2bnst", [128, 48], F32, side="right"))
            qct_ps = [E2(nc.psum_tensor(f"qctps{i}", [128, 512], F32))
                      for i in range(2)]
            trps2 = [E2(nc.psum_tensor(f"p2trps{i}", [128, 512], F32))
                    for i in range(2)]
            ssq_ps = E2(nc.psum_tensor("ssqps", [65, C.T_own], F32))
            rB_ps = E2(nc.psum_tensor("rBps", [128, C.T_own], F32))

            P.op("sync", lambda e: e.dma_start(
                out=g_qaT[:], in_=g_qa.rearrange("(c p) -> p c", p=128),
                allow_slow_non_contiguous=True),
                inc="qc1", incn=16)
            P.op("sync", lambda e: e.dma_start(
                out=SvT[:], in_=Sv.rearrange("(c p) -> p c", p=128),
                allow_slow_non_contiguous=True),
                inc="qc1", incn=16)
            qx_cst = P.c["qc1"]

            xf_free = [(), (), ()]
            xn_free = [(), ()]
            trps_free = [(), ()]
            xnT_ev = []
            norm_of2 = {}
            v6_of2 = {}
            prev_norm2 = [0]
            wcol0_ssq = [0]

            def emit_ln2(t):
                i = t % 2
                i2 = 0
                qx_t = P.op("gpsimd", lambda e, t=t, i=i: e.dma_start(
                    out=xf2[i][:], in_=x[t * 128:(t + 1) * 128, :]),
                    waits=xf_free[i], inc=f"qg{i}", incn=16)
                for g in range(C.D // 512):
                    P.op("vector", lambda e, i=i, g=g: e.bn_stats(
                        bnst2[:, 6 * g:6 * g + 6],
                        xf2[i][:, 512 * g:512 * g + 512]),
                        waits=[(f"qg{i}", qx_t)], inc="V")
                v_ag = P.op("vector", lambda e: e.bn_aggr(
                    stat2[:, 0:2], bnst2[:, 0:6 * (C.D // 512)]),
                    waits=[("S", prev_norm2[0])], inc="V")
                s_sd = P.op("scalar", lambda e: e.activation(
                    stat2[:, 3:4], stat2[:, 1:2], AF.Sqrt,
                    bias=C.EPS_LN), waits=[("V", v_ag)], inc="S")
                P.op("vector", lambda e: e.reciprocal(
                    stat2[:, 5:6], stat2[:, 3:4]),
                    waits=[("S", s_sd)], inc="V")
                v6 = P.op("vector", lambda e: e.scalar_tensor_tensor(
                    stat2[:, 6:7], stat2[:, 0:1], -1.0, stat2[:, 5:6],
                    ALU.mult, ALU.mult), inc="V")
                v6_of2[t] = v6
                wcol0_ssq[0] = v6

            def emit_norm2(t):
                i = t % 2
                i2 = 0
                s_norm = P.op("scalar", lambda e, i=i, i2=i2: e.activation(
                    xn2[i2][:], xf2[i][:], AF.Identity,
                    scale=stat2[:, 5:6], bias=stat2[:, 6:7]),
                    waits=[("V", v6_of2[t])] + list(xn_free[i2]), inc="S")
                xf_free[i] = (("S", s_norm),)
                norm_of2[t] = s_norm
                prev_norm2[0] = s_norm

            emit_ln2(0)
            emit_norm2(0)
            for t in range(C.TSO):
                i2 = 0
                if t + 1 < C.TSO:
                    emit_ln2(t + 1)
                s_norm = norm_of2[t]
                last_ev = 0
                for g in range(C.DC // 4):
                    b = g % 2
                    t_tr = None
                    for k in range(4):
                        c = g * 4 + k
                        t_tr = P.op("tensor", lambda e, i2=i2, c=c, b=b, k=k:
                                    e.matmul(
                            trps2[b][:, (c % 4) * 128:(c % 4) * 128 + 128],
                            xn2[i2][:, c * 128:(c + 1) * 128], ident_bf[:],
                            start=(k == 0), stop=(k == 3), skip_group_check=True),
                            waits=[("S", s_norm)]
                            + (list(trps_free[b]) if k == 0 else []),
                            inc="T" if k == 3 else None)
                    ev = P.op("vector", lambda e, t=t, g=g, b=b: e.tensor_copy(
                        xnT[:].rearrange("p (c n) -> p c n", c=C.DC)
                        [:, g * 4:(g + 1) * 4, t * 128:(t + 1) * 128],
                        trps2[b][:].rearrange("p (c n) -> p c n", c=4)),
                        waits=[("T", t_tr)], inc="V")
                    trps_free[b] = (("V", ev),)
                    last_ev = ev
                xn_free[i2] = (("V", last_ev),)
                if t + 1 < C.TSO:
                    emit_norm2(t + 1)
            xnT_done_v = P.c["V"]

            WQ = ["qg4", "qg5", "qg2"]
            NWB = 3 if C.QRC >= 3 else 2
            wcol_free = [(), (), ()]
            wcol_free[2] = ((WQ[2], P.c[WQ[2]]),)
            qct_free = [(), ()]
            ssq_last_mm = 0
            for qrc in range(C.QRC):
                wb = qrc % NWB
                qw_w = P.op("gpsimd", lambda e, qrc=qrc, wb=wb: e.dma_start(
                    out=wcol[wb][:].rearrange("p (c m) -> p c m", c=C.DC),
                    in_=w_qa[:, qrc * 128:(qrc + 1) * 128].rearrange(
                        "(c p) m -> p c m", p=128)),
                    waits=wcol_free[wb], inc=WQ[wb], incn=16)
                for qn in range(C.NQC):
                    pb = qn % 2
                    t_q = None
                    for c in range(C.DC):
                        t_q = P.op("tensor", lambda e, c=c, qn=qn, wb=wb, pb=pb:
                                   e.matmul(
                            qct_ps[pb][:],
                            wcol[wb][:, c * 128:(c + 1) * 128],
                            xnT[:, c * C.T_own + qn * 512:
                                c * C.T_own + qn * 512 + 512],
                            start=(c == 0), stop=(c == C.DC - 1)),
                            waits=([(WQ[wb], qw_w), ("V", xnT_done_v)]
                                   + list(qct_free[pb])) if c == 0 else [],
                            inc="T" if c == C.DC - 1 else None)
                    s_sq = P.op("scalar", lambda e, pb=pb: e.activation(
                        qsq[:], qct_ps[pb][:], AF.Square),
                        waits=[("T", t_q)], inc="S")
                    v_raw = P.op("vector", lambda e, qrc=qrc, qn=qn, pb=pb:
                                 e.tensor_scalar(
                        q_cnT[:, qrc * C.T_own + qn * 512:
                              qrc * C.T_own + qn * 512 + 512],
                        qct_ps[pb][:], g_qaT[:, qrc:qrc + 1], None, ALU.mult),
                        waits=[("S", s_sq), ("qc1", qx_cst)], inc="V")
                    qct_free[pb] = (("V", v_raw),)
                    ssq_last_mm = P.op("tensor", lambda e, qrc=qrc, qn=qn:
                                       e.matmul(
                        ssq_ps[0:65, qn * 512:qn * 512 + 512],
                        ones_bf[:, 0:65], qsq[:],
                        start=(qrc == 0), stop=(qrc == C.QRC - 1),
                        skip_group_check=True),
                        waits=[("S", s_sq)], inc="T")
                wcol_free[wb] = (("T", ssq_last_mm),)
            s_r = P.op("scalar", lambda e: e.activation(
                rrow[:], ssq_ps[0:1, :], AF.Sqrt, scale=1.0 / C.QR),
                waits=[("T", ssq_last_mm)], inc="S")
            P.op("vector", lambda e: e.tensor_scalar(
                rrow[:], rrow[:], C.EPS_RMS, None, ALU.add),
                waits=[("S", s_r)], inc="V")
            v_rr = P.op("vector", lambda e: e.reciprocal(rrow[:], rrow[:]), inc="V")
            t_rb = None
            for (noff, nlen) in _nchunks(C.T_own):
                t_rb = P.op("tensor", lambda e, noff=noff, nlen=nlen: e.matmul(
                    rB_ps[:, noff:noff + nlen], ones_f32[0:1, :],
                    rrow[:, noff:noff + nlen], start=True, stop=True),
                    waits=[("V", v_rr)], inc="T")
            P.op("vector", lambda e: e.tensor_copy(rstdB[:], rB_ps[:]),
                 waits=[("T", t_rb)], inc="V")

            h1_free = [(), ()]
            for rc in range(C.RC):
                wb = (C.QRC + rc) % NWB
                qw_w = P.op("gpsimd", lambda e, rc=rc, wb=wb: e.dma_start(
                    out=wcol[wb][:].rearrange("p (c m) -> p c m", c=C.DC),
                    in_=V[:, rc * 128:(rc + 1) * 128].rearrange(
                        "(c p) m -> p c m", p=128)),
                    waits=wcol_free[wb], inc=WQ[wb], incn=16)
                for tn in range(C.NQC):
                    pb = tn % 2
                    t_h = None
                    for c in range(C.DC):
                        t_h = P.op("tensor", lambda e, c=c, tn=tn, wb=wb, pb=pb:
                                   e.matmul(
                            qct_ps[pb][:],
                            wcol[wb][:, c * 128:(c + 1) * 128],
                            xnT[:, c * C.T_own + tn * 512:
                                c * C.T_own + tn * 512 + 512],
                            start=(c == 0), stop=(c == C.DC - 1)),
                            waits=([(WQ[wb], qw_w)] + list(h1_free[pb]))
                            if c == 0 else [],
                            inc="T" if c == C.DC - 1 else None)
                    v_h = P.op("vector", lambda e, rc=rc, tn=tn, pb=pb:
                               e.tensor_scalar(
                        h1sT[:, rc * C.T_own + tn * 512:
                             rc * C.T_own + tn * 512 + 512],
                        qct_ps[pb][:], SvT[:, rc:rc + 1], None, ALU.mult),
                        waits=[("T", t_h)], inc="V")
                    h1_free[pb] = (("V", v_h),)
                wcol_free[wb] = (("T", P.c["T"]),)
            P.barrier()

        if phases < 3:
            ph123.close()
            _early_exit(P, nc, out, const_sb)
            return _finish(nc, top, P)
        oT = E(nc.sbuf_tensor("oT", [128, C.NH * C.T_own], BF16, side="left"))
        with ExitStack() as ph3:
            E3 = ph3.enter_context
            rope2 = E3(nc.sbuf_tensor("rope2", [64, C.T_batch], BF16,
                                      side="right"))
            w_qb_bf = [E3(nc.sbuf_tensor(f"wqb{i}", [128, C.QRC * C.HW], BF16,
                                         side="right")) for i in range(2)]
            w_kvb_bf = [E3(nc.sbuf_tensor(f"wkvb{i}", [128, C.KVC * C.KW], BF16,
                                          side="right")) for i in range(2)]
            knopeT = E3(nc.sbuf_tensor("knopeT", [128, C.T_batch], BF16,
                                       side="right"))
            V1h = E3(nc.sbuf_tensor("V1h", [128, C.T_batch], BF16, side="right"))
            PT = [E3(nc.sbuf_tensor(f"PT{i}", [128, 512], BF16, side="right"))
                  for i in range(4)]
            qnopeT = E3(nc.sbuf_tensor("qnopeT", [128, C.T_own], BF16, side="right"))
            qropeM = E3(nc.sbuf_tensor("qropeM", [65, C.T_own], BF16, side="right"))
            qrtmp = E3(nc.sbuf_tensor("qrtmp", [64, C.T_own], BF16, side="right"))
            sq2 = E3(nc.sbuf_tensor("sq2", [128, max(1024, C.T_batch // 2)], BF16,
                                    side="right"))
            mrow = E3(nc.sbuf_tensor("mrow", [1, 16], F32, side="right"))
            bias_col = E3(nc.sbuf_tensor("bias_col", [128, 1], F32, side="right"))
            lrec_bf = E3(nc.sbuf_tensor("lrec_bf", [1, 512], BF16, side="right"))
            recipB = E3(nc.sbuf_tensor("recipB", [128, 512], BF16, side="right"))
            kx_ps = [E3(nc.psum_tensor(f"kxps{i}", [128, 512], F32))
                     for i in range(2)]
            s_ps = [E3(nc.psum_tensor(f"sps{i}", [128, 512], F32))
                    for i in range(4)]
            av_ps = E3(nc.psum_tensor("avps", [128, 512], F32))
            l_ps = E3(nc.psum_tensor("lps", [65, 512], F32))

            wqb_free = [(), ()]
            wkvb_free = [(("qg2", P.c["qg2"]),), (("qg3", P.c["qg3"]),)]
            kx_free = [(), ()]
            v_r2 = P.op("vector", lambda e: e.tensor_tensor(
                rope2[:], ropeP[0:64, :], ropeP[0:64, :], ALU.mult), inc="V")
            s_free = [(), (), (), ()]
            qsq3_free = ()
            sq2_slot_free = [(), ()]
            qsq3 = E3(nc.sbuf_tensor("qsq3", [128, 512], BF16, side="right"))
            PT_free = [(), (), (), ()]
            av_free = ()
            l_free = ()
            pending_fin = [None]

            def flush_fin(bank=0):
                if pending_fin[0] is not None:
                    pending_fin[0](bank)
                    pending_fin[0] = None
            knopeT_free = ()
            V1h_free = ()
            q_free = ()

            qw_head = [0, 0]
            for h in range(C.NH):
                wb = h % 2
                qw_head[wb] = P.op("gpsimd", lambda e, h=h, wb=wb: e.dma_start(
                    out=w_qb_bf[wb][:].rearrange("p (c m) -> p c m", c=C.QRC),
                    in_=w_qb[:, h * C.HW:(h + 1) * C.HW].rearrange(
                        "(c p) m -> p c m", p=128)),
                    waits=wqb_free[wb], inc=f"qg{wb}", incn=16)
                qw_kvb = P.op("gpsimd", lambda e, h=h, wb=wb: e.dma_start(
                    out=w_kvb_bf[wb][:].rearrange("p (c m) -> p c m", c=C.KVC),
                    in_=w_kvb[:, h * C.KW:(h + 1) * C.KW].rearrange(
                        "(c p) m -> p c m", p=128)),
                    waits=wkvb_free[wb], inc=f"qg{2 + wb}", incn=16)

                NKN = C.T_batch // 512
                kn_ev = [0] * NKN
                for n in range(NKN):
                    pb = n % 2
                    t_k = None
                    for c in range(C.KVC):
                        t_k = P.op("tensor", lambda e, c=c, n=n, wb=wb, pb=pb:
                                   e.matmul(
                            kx_ps[pb][:],
                            w_kvb_bf[wb][:, c * C.KW:c * C.KW + C.ND],
                            k_cnT[:, c * C.T_batch + n * 512:
                                  c * C.T_batch + n * 512 + 512],
                            start=(c == 0), stop=(c == C.KVC - 1)),
                            waits=([(f"qg{2 + wb}", qw_kvb)] + list(kx_free[pb])
                                   + (list(knopeT_free) if n == 0 else []))
                            if c == 0 else [],
                            inc="T" if c == C.KVC - 1 else None)
                    v_ev = P.op("vector", lambda e, n=n, pb=pb: e.tensor_copy(
                        knopeT[:, n * 512:(n + 1) * 512], kx_ps[pb][:]),
                        waits=[("T", t_k)], inc="V")
                    kx_free[pb] = (("V", v_ev),)
                    kn_ev[n] = v_ev
                    if n == 0:
                        flush_fin(bank=1)

                last_vev = 0
                for grp in range(C.TSB // 4):
                    pb = grp % 2
                    t_v = None
                    for slot in range(4):
                        ts = grp * 4 + slot
                        for c in range(C.KVC):
                            t_v = P.op("tensor", lambda e, c=c, ts=ts, slot=slot,
                                 wb=wb, pb=pb: e.matmul(
                                kx_ps[pb][:, slot * 128:(slot + 1) * 128],
                                k_cnT[:, c * C.T_batch + ts * 128:
                                      c * C.T_batch + ts * 128 + 128],
                                w_kvb_bf[wb][:, c * C.KW + C.ND:
                                             c * C.KW + C.ND + C.VD],
                                start=(slot == 0 and c == 0),
                                stop=(slot == 3 and c == C.KVC - 1),
                                skip_group_check=True),
                                waits=(list(kx_free[pb])
                                       + (list(V1h_free) if grp == 0 else []))
                                if (c == 0 and slot == 0) else [],
                                inc="T" if (slot == 3 and c == C.KVC - 1)
                                else None)
                    last_vev = P.op("vector", lambda e, grp=grp, pb=pb:
                                    e.tensor_copy(
                        V1h[:, grp * 512:(grp + 1) * 512], kx_ps[pb][:]),
                        waits=[("T", t_v)], inc="V")
                    kx_free[pb] = (("V", last_vev),)
                NKB = NKN // 2 if NKN % 2 == 0 else NKN
                wide = (NKN % 2 == 0)
                kw = 1024 if wide else 512
                s_k2_of = {}

                def emit_k2(nb, kn_ev=kn_ev):
                    sl = nb % 2
                    s_k2_of[nb] = P.op(
                        "scalar", lambda e, nb=nb, sl=sl, kw=kw: e.activation(
                            sq2[:, sl * kw:(sl + 1) * kw],
                            knopeT[:, nb * kw:(nb + 1) * kw], AF.Square),
                        waits=[("V", kn_ev[min((nb + 1) * (2 if wide else 1),
                                               NKN) - 1])]
                        + list(sq2_slot_free[sl]), inc="S")

                emit_k2(0)
                if NKB > 1:
                    emit_k2(1)
                t_qn = [None, None]
                t_qr = [None, None]
                for qn in range(C.NQC):
                    for c in range(C.QRC):
                        t_qn[qn] = P.op("tensor", lambda e, c=c, qn=qn, wb=wb:
                                        e.matmul(
                            s_ps[qn][:],
                            w_qb_bf[wb][:, c * C.HW:c * C.HW + C.ND],
                            q_cnT[:, c * C.T_own + qn * 512:
                                  c * C.T_own + qn * 512 + 512],
                            start=(c == 0), stop=(c == C.QRC - 1)),
                            waits=([(f"qg{wb}", qw_head[wb])] + list(s_free[qn])
                                   + list(q_free)) if c == 0 else [],
                            inc="T" if c == C.QRC - 1 else None)
                        t_qr[qn] = P.op("tensor", lambda e, c=c, qn=qn, wb=wb:
                                        e.matmul(
                            s_ps[2 + qn][0:64, :],
                            w_qb_bf[wb][:, c * C.HW + C.ND:c * C.HW + C.HW],
                            q_cnT[:, c * C.T_own + qn * 512:
                                  c * C.T_own + qn * 512 + 512],
                            start=(c == 0), stop=(c == C.QRC - 1)),
                            waits=list(s_free[2 + qn]) if c == 0 else [],
                            inc="T" if c == C.QRC - 1 else None)
                ev_qn = [None, None]
                ev_qr = [None, None]
                for qn in range(C.NQC):
                    ev_qn[qn] = P.op("vector", lambda e, qn=qn: e.tensor_tensor(
                        qnopeT[:, qn * 512:qn * 512 + 512], s_ps[qn][:],
                        rstdB[:, qn * 512:qn * 512 + 512], ALU.mult),
                        waits=[("T", t_qn[qn])], inc="V")
                    ev_qr[qn] = P.op("vector", lambda e, qn=qn: e.tensor_tensor(
                        qrtmp[0:64, qn * 512:qn * 512 + 512],
                        s_ps[2 + qn][0:64, :],
                        rstdB[0:64, qn * 512:qn * 512 + 512], ALU.mult),
                        waits=[("T", t_qr[qn])], inc="V")
                q_nope_v = max(x for x in [ev_qn[0], ev_qn[C.NQC - 1]])
                for qn in range(C.NQC):
                    v_s1 = P.op("scalar", lambda e, qn=qn: e.activation(
                        qsq3[:, 0:512], qnopeT[:, qn * 512:qn * 512 + 512],
                        AF.Square),
                        waits=[("V", ev_qn[qn])] + list(qsq3_free), inc="S")
                    t_n1 = P.op("tensor", lambda e, qn=qn: e.matmul(
                        s_ps[qn][0:65, :], ones_bf[:, 0:65], qsq3[:, 0:512],
                        start=True, stop=False, skip_group_check=True),
                        waits=[("S", v_s1), ("V", ev_qn[qn])]
                        + list(s_free[qn]), inc="T")
                    v_s2 = P.op("scalar", lambda e, qn=qn: e.activation(
                        qsq3[0:64, 0:512], qrtmp[0:64, qn * 512:qn * 512 + 512],
                        AF.Square),
                        waits=[("V", ev_qr[qn]), ("T", t_n1)], inc="S")
                    t_n = P.op("tensor", lambda e, qn=qn: e.matmul(
                        s_ps[qn][0:65, :], ones_bf[0:64, 0:65],
                        qsq3[0:64, 0:512],
                        start=False, stop=True, skip_group_check=True),
                        waits=[("S", v_s2)], inc="T")
                    v_m = P.op("vector", lambda e, qn=qn: e.tensor_scalar(
                        qropeM[64:65, qn * 512:qn * 512 + 512],
                        s_ps[qn][64:65, :],
                        -0.5, None, ALU.mult), waits=[("T", t_n)], inc="V")
                    s_free[qn] = (("V", v_m),)
                    qsq3_free = (("T", t_n),)
                t_pm = [None, None]
                ev_pm = [None, None]
                for qn in range(C.NQC):
                    t_pm[qn] = P.op("tensor", lambda e, qn=qn: e.matmul(
                        s_ps[2 + qn][0:64, :], perm_bf[:],
                        qrtmp[0:64, qn * 512:qn * 512 + 512],
                        start=True, stop=True),
                        waits=[("V", ev_qr[qn])], inc="T")
                    ev_pm[qn] = P.op("vector", lambda e, qn=qn: e.tensor_copy(
                        qropeM[0:64, qn * 512:qn * 512 + 512],
                        s_ps[2 + qn][0:64, :]),
                        waits=[("T", t_pm[qn])], inc="V")
                    s_free[2 + qn] = (("V", ev_pm[qn]),)
                v_c2 = P.op("vector", lambda e: e.tensor_tensor(
                    qropeM[0:64, :], qropeM[0:64, :], sinQ[:], ALU.mult),
                    inc="V")
                v_c1 = P.op("vector", lambda e: e.tensor_tensor(
                    qrtmp[:], qrtmp[:], cosQ[:], ALU.mult),
                    waits=[("S", P.c["S"])], inc="V")
                v_rot = P.op("vector", lambda e: e.tensor_tensor(
                    qropeM[0:64, :], qropeM[0:64, :], qrtmp[:], ALU.add),
                    inc="V")
                for nb in range(NKB):
                    sl = nb % 2
                    s_k2 = s_k2_of[nb]
                    t_kn = None
                    for h2 in range(2 if wide else 1):
                        b2 = (nb * (2 if wide else 1) + h2) % 2
                        P.op("tensor", lambda e, nb=nb, sl=sl, h2=h2, kw=kw,
                             b2=b2: e.matmul(
                            s_ps[b2][0:65, :], ones_bf[:, 0:65],
                            sq2[:, sl * kw + h2 * 512:sl * kw + h2 * 512 + 512],
                            start=True, stop=False, skip_group_check=True),
                            waits=[("S", s_k2)] + list(s_free[b2]))
                        t_kn = P.op("tensor", lambda e, nb=nb, h2=h2, kw=kw,
                                    b2=b2: e.matmul(
                            s_ps[b2][0:65, :], ones_bf[0:64, 0:65],
                            rope2[:, nb * kw + h2 * 512:
                                  nb * kw + h2 * 512 + 512],
                            start=False, stop=True, skip_group_check=True),
                            inc="T")
                        v_mx = P.op("vector", lambda e, nb=nb, h2=h2, b2=b2:
                                    e.reduce_max(
                            mrow[:, nb * (2 if wide else 1) + h2:
                                 nb * (2 if wide else 1) + h2 + 1],
                            s_ps[b2][0:1, :], axis=AX.X),
                            waits=[("T", t_kn)], inc="V")
                        s_free[b2] = (("V", v_mx),)
                    sq2_slot_free[sl] = (("T", t_kn),)
                    if nb + 2 < NKB:
                        emit_k2(nb + 2)
                v_mx2 = P.op("vector", lambda e: e.reduce_max(
                    mrow[:, 15:16], mrow[:, 0:NKN], axis=AX.X), inc="V")
                q_ready_v = P.c["V"]
                v_bc = None

                def emit_bias():
                    nonlocal v_bc
                    t_bc = P.op("tensor", lambda e: e.matmul(
                        s_ps[1][:, 0:1], negsc[0:1, :], mrow[0:1, 15:16],
                        start=True, stop=True, skip_group_check=True),
                        waits=[("V", v_mx2)] + list(s_free[1]), inc="T")
                    v_bc = P.op("vector", lambda e: e.tensor_copy(
                        bias_col[:], s_ps[1][:, 0:1]),
                        waits=[("T", t_bc)], inc="V")
                    s_free[1] = (("V", v_bc),)

                for qc in range(C.NQC):
                    exp_done = [0] * C.KS
                    t_avl = [0] * C.KS

                    def emit_av(j, qc=qc):
                        sb = j % 4
                        P.op("tensor", lambda e, j=j, sb=sb: e.matmul(
                            l_ps[0:65, :], ones_bf[:, 0:65], PT[sb][:],
                            start=(j == 0), stop=(j == C.KS - 1),
                            skip_group_check=True),
                            waits=[("S", exp_done[j])]
                            + (list(l_free) if j == 0 else []))
                        t_avl[j] = P.op("tensor", lambda e, j=j, sb=sb, qc=qc:
                                        e.matmul(
                            av_ps[:],
                            V1h[:, j * 128:(j + 1) * 128],
                            PT[sb][:],
                            start=(j == 0), stop=(j == C.KS - 1),
                            skip_group_check=True),
                            waits=([("V", last_vev)] + list(av_free))
                            if j == 0 else [],
                            inc="T")
                        PT_free[sb] = (("T", t_avl[j]),)

                    for ks in range(C.KS):
                        sb = ks % 4
                        P.op("tensor", lambda e, ks=ks, sb=sb, qc=qc: e.matmul(
                            s_ps[sb][:],
                            knopeT[:, ks * 128:(ks + 1) * 128],
                            qnopeT[:, qc * 512:qc * 512 + 512],
                            start=True, stop=False),
                            waits=(([("V", q_nope_v)] if ks == 0 else [])
                                   + (list(s_free[sb]) if ks < 4 else [])))
                        t_s = P.op("tensor", lambda e, ks=ks, sb=sb, qc=qc:
                                   e.matmul(
                            s_ps[sb][:],
                            ropeP[:, ks * 128:(ks + 1) * 128],
                            qropeM[:, qc * 512:qc * 512 + 512],
                            start=False, stop=True),
                            waits=[("V", q_ready_v)] if ks == 0 else [],
                            inc="T")
                        if qc == 0 and ks == 0:
                            emit_bias()
                        exp_done[ks] = P.op("scalar", lambda e, sb=sb:
                                            e.activation(
                            PT[sb][:], s_ps[sb][:], AF.Exp,
                            scale=C.SCALE, bias=bias_col[:]),
                            waits=[("T", t_s)]
                            + ([("V", v_bc)] if ks == 0 else [])
                            + (list(PT_free[sb]) if ks < 4 else []), inc="S")
                        s_free[sb] = (("S", exp_done[ks]),)
                        if ks == 2:
                            flush_fin()
                        if ks >= 3:
                            emit_av(ks - 3)
                    emit_av(C.KS - 3)
                    emit_av(C.KS - 2)
                    emit_av(C.KS - 1)
                    t_att = P.c["T"]

                    def _fin(bank, h=h, qc=qc, t_att=t_att):
                        nonlocal av_free, l_free, kx_free
                        s_ln = P.op("scalar", lambda e: e.activation(
                            l_ps[64:65, :], l_ps[0:1, :], AF.Ln),
                            waits=[("T", t_att)], inc="S")
                        v_lr = P.op("scalar", lambda e: e.activation(
                            lrec_bf[:], l_ps[64:65, :], AF.Exp, scale=-1.0),
                            inc="S")
                        if bank == 0:
                            bps, bfree = kx_ps[1], kx_free[1]
                        else:
                            bps, bfree = s_ps[1], s_free[1]
                        t_rb = P.op("tensor", lambda e, bps=bps: e.matmul(
                            bps[:], ones_bf[0:1, :], lrec_bf[:],
                            start=True, stop=True, skip_group_check=True),
                            waits=[("S", v_lr)] + list(bfree), inc="T")
                        v_rb = P.op("vector", lambda e, bps=bps: e.tensor_copy(
                            recipB[:], bps[:]), waits=[("T", t_rb)], inc="V")
                        if bank == 0:
                            kx_free[1] = (("V", v_rb),)
                        else:
                            s_free[1] = (("V", v_rb),)
                        v_o = P.op("vector", lambda e, h=h, qc=qc:
                                   e.tensor_tensor(
                            oT[:, h * C.T_own + qc * 512:
                               h * C.T_own + qc * 512 + 512],
                            av_ps[:], recipB[:], ALU.mult), inc="V")
                        av_free = (("V", v_o),)
                        l_free = (("S", v_lr),)
                    pending_fin[0] = _fin
                wqb_free[wb] = (("T", P.c["T"]),)
                wkvb_free[wb] = (("T", P.c["T"]),)
                knopeT_free = (("T", P.c["T"]),)
                V1h_free = (("T", P.c["T"]),)
                q_free = (("T", P.c["T"]),)
            flush_fin()
            P.barrier()
        ph123.close()

        if phases < 4:
            _early_exit(P, nc, out, const_sb)
            return _finish(nc, top, P)
        with ExitStack() as ph4:
            E4 = ph4.enter_context
            wo_buf = [E4(nc.sbuf_tensor(f"wo{i}", [128, C.NH * 512], BF16,
                                        side="right")) for i in range(2)]
            u_tmp = E4(nc.sbuf_tensor("u_tmp", [128, 4 * C.RANK], BF16,
                                      side="right"))
            uT = [E4(nc.sbuf_tensor(f"uT{i}", [128, C.RC * 512], BF16,
                                    side="right")) for i in range(2)]
            xres = [E4(nc.sbuf_tensor(f"xres{i}", [128, 512], F32, side="right"))
                    for i in range(2)]
            stage = [E4(nc.sbuf_tensor(f"stage{i}", [128, 512], F32, side="right"))
                     for i in range(2)]
            out_ps = [E4(nc.psum_tensor(f"outps{i}", [128, 512], F32))
                      for i in range(2)]
            trps4 = [E4(nc.psum_tensor(f"p4trps{i}", [128, 512], F32))
                    for i in range(2)]

            wo_free = [(), ()]
            uT_free = [(), ()]
            utmp_free = ()
            xres_free = [(), ()]
            stage_free = [(), ()]
            outps_free = [(), ()]
            trps_free = [(), ()]
            for j in range(C.NJ):
                wb = j % 2
                qw_wo = P.op("gpsimd", lambda e, j=j, wb=wb: e.dma_start(
                    out=wo_buf[wb][:].rearrange("p (c m) -> p c m", c=C.NH),
                    in_=w_o[:, j * 512:(j + 1) * 512].rearrange(
                        "(c p) m -> p c m", p=128)),
                    waits=wo_free[wb], inc=f"qg{wb}", incn=16)
                qw_u = P.op("gpsimd", lambda e, j=j: e.dma_start(
                    out=u_tmp[:].rearrange("p (c m) -> p c m", c=4),
                    in_=U[j * 512:(j + 1) * 512, :].rearrange(
                        "(c p) m -> p c m", p=128)),
                    waits=utmp_free, inc="qc0", incn=16)
                n_tr = 0
                uev = []
                for rb in range(C.RC):
                    pb = rb % 2
                    t_tr = None
                    for c4 in range(4):
                        t_tr = P.op("tensor", lambda e, rb=rb, c4=c4, pb=pb:
                                    e.matmul(
                            trps4[pb][:, c4 * 128:(c4 + 1) * 128],
                            u_tmp[:, c4 * C.RANK + rb * 128:
                                  c4 * C.RANK + rb * 128 + 128],
                            ident_bf[:], start=(c4 == 0), stop=(c4 == 3),
                            skip_group_check=True),
                            waits=([("qc0", qw_u)] + list(trps_free[pb])
                                   + (list(uT_free[wb]) if rb == 0 else []))
                            if c4 == 0 else [],
                            inc="T" if c4 == 3 else None)
                    ev = P.op("vector", lambda e, rb=rb, pb=pb, wb=wb:
                              e.tensor_copy(
                        uT[wb][:, rb * 512:(rb + 1) * 512], trps4[pb][:]),
                        waits=[("T", t_tr)], inc="V")
                    trps_free[pb] = (("V", ev),)
                    uev.append(ev)
                uT_ready = uev[-1]
                utmp_free = (("T", P.c["T"]),)
                for tc in range(C.TSO):
                    pb = tc % 2
                    qx_x = P.op("sync", lambda e, j=j, tc=tc, pb=pb: e.dma_start(
                        out=xres[pb][:],
                        in_=x[tc * 128:(tc + 1) * 128, j * 512:(j + 1) * 512]),
                        waits=xres_free[pb], inc=f"qs{2 + pb}", incn=16)
                    for c in range(C.NH):
                        P.op("tensor", lambda e, c=c, j=j, tc=tc, wb=wb, pb=pb:
                             e.matmul(
                            out_ps[pb][:],
                            oT[:, c * C.T_own + tc * 128:
                               c * C.T_own + tc * 128 + 128],
                            wo_buf[wb][:, c * 512:(c + 1) * 512],
                            start=(c == 0), stop=False),
                            waits=([(f"qg{wb}", qw_wo)] + list(outps_free[pb]))
                            if c == 0 else [])
                    t_o = None
                    for rc in range(C.RC):
                        t_o = P.op("tensor", lambda e, rc=rc, tc=tc, wb=wb, pb=pb:
                             e.matmul(
                            out_ps[pb][:],
                            h1sT[:, rc * C.T_own + tc * 128:
                                 rc * C.T_own + tc * 128 + 128],
                            uT[wb][:, rc * 512:(rc + 1) * 512],
                            start=False, stop=(rc == C.RC - 1)),
                            waits=[("V", uT_ready)] if rc == 0 else [],
                            inc="T" if rc == C.RC - 1 else None)
                    v_st = P.op("vector", lambda e, pb=pb: e.tensor_tensor(
                        stage[pb][:], out_ps[pb][:], xres[pb][:], ALU.add),
                        waits=[("T", t_o), (f"qs{2 + pb}", qx_x)] + list(stage_free[pb]),
                        inc="V")
                    outps_free[pb] = (("V", v_st),)
                    xres_free[pb] = (("V", v_st),)
                    qo_st = P.op("sync", lambda e, j=j, tc=tc, pb=pb: e.dma_start(
                        out=out[tc * 128:(tc + 1) * 128, j * 512:(j + 1) * 512],
                        in_=stage[pb][:]),
                        waits=[("V", v_st)], inc=f"qo{pb}", incn=16)
                    stage_free[pb] = ((f"qo{pb}", qo_st),)
                uT_free[wb] = (("T", P.c["T"]),)
                wo_free[wb] = (("T", P.c["T"]),)

            P.wait("sync", ("qo0", P.c["qo0"]), ("qo1", P.c["qo1"]))

        return _finish(nc, top, P)


def _early_exit(P, nc, out, const_sb):
    P.barrier()
    qo = P.op("sync", lambda e: e.dma_start(
        out=out[0:128, 0:8], in_=const_sb[:, 0:8]), inc="qo0", incn=16)
    P.wait("sync", ("qo0", qo))


def _finish(nc, top, P):
    with nc.Block() as block:
        block.sync(lambda e: [f(e) for f in P.sched["sync"]])
        block.gpsimd(lambda e: [f(e) for f in P.sched["gpsimd"]])
        block.vector(lambda e: [f(e) for f in P.sched["vector"]])
        block.scalar(lambda e: [f(e) for f in P.sched["scalar"]])
        block.tensor(lambda e: [f(e) for f in P.sched["tensor"]])
    top.close()
    return nc


def make_tables(cfg: Cfg):
    t = np.arange(cfg.T_batch, dtype=np.float64)
    inv = 1.0 / (10000.0 ** (np.arange(0, cfg.RD, 2, dtype=np.float64) / cfg.RD))
    freqs = np.outer(t, inv)
    emb = np.concatenate([freqs, freqs], axis=-1)
    cos = np.cos(emb).T.astype(np.float32)
    sin = np.sin(emb).T.astype(np.float32)
    return np.ascontiguousarray(cos), np.ascontiguousarray(sin)


def make_consts():
    ident = np.eye(128, dtype=np.float32)
    onesm = np.ones((128, 128), dtype=np.float32)
    perm = np.zeros((64, 64), dtype=np.float32)
    for i in range(32):
        perm[i + 32, i] = -1.0
        perm[i, i + 32] = 1.0
    return ident, onesm, perm


def make_in_maps(cfg, weights, x_full, n_rot):
    cos, sin = make_tables(cfg)
    ident, onesm, perm = make_consts()
    maps = []
    for r in n_rot:
        xr = np.ascontiguousarray(np.roll(x_full, -r, axis=0))
        cr = np.ascontiguousarray(np.roll(cos, -r, axis=1))
        sr = np.ascontiguousarray(np.roll(sin, -r, axis=1))
        m = dict(weights)
        m.update(x=xr, cosT=cr, sinT=sr, ident=ident, ones=onesm, perm=perm)
        maps.append(m)
    return maps


def kernel(L0_attn_hidden_states, L0_attn_w_qa, L0_attn_g_qa, L0_attn_w_qb,
           L0_attn_w_kva, L0_attn_g_kva, L0_attn_w_kvb, L0_attn_w_o,
           L0_ffn_U, L0_ffn_S, L0_ffn_V):
    cfg = FULL
    nc = build_program(cfg)
    weights = {
        "w_qa": np.ascontiguousarray(L0_attn_w_qa, dtype=np.float32),
        "g_qa": np.ascontiguousarray(L0_attn_g_qa, dtype=np.float32),
        "w_qb": np.ascontiguousarray(L0_attn_w_qb, dtype=np.float32),
        "w_kva": np.ascontiguousarray(L0_attn_w_kva, dtype=np.float32),
        "g_kva": np.ascontiguousarray(L0_attn_g_kva, dtype=np.float32),
        "w_kvb": np.ascontiguousarray(L0_attn_w_kvb, dtype=np.float32),
        "w_o": np.ascontiguousarray(L0_attn_w_o, dtype=np.float32),
        "U": np.ascontiguousarray(L0_ffn_U, dtype=np.float32),
        "Sv": np.ascontiguousarray(L0_ffn_S, dtype=np.float32),
        "V": np.ascontiguousarray(L0_ffn_V, dtype=np.float32),
    }
    x = np.asarray(L0_attn_hidden_states, dtype=np.float32)
    in_maps = []
    for core in range(8):
        b, jq = core // 4, core % 4
        in_maps.extend(make_in_maps(cfg, weights, x[b], [jq * cfg.T_own]))
    res = run_bass_kernel_spmd(nc, in_maps, core_ids=list(range(8)))
    outp = np.empty((2, 4096, 4096), dtype=np.float32)
    for core in range(8):
        b, jq = core // 4, core % 4
        outp[b, jq * cfg.T_own:(jq + 1) * cfg.T_own, :] = res.results[core]["out"]
    return outp
